# version 02
# baseline (speedup 1.0000x reference)
import contextlib

import numpy as np
import ml_dtypes

import concourse.bass as bass
import concourse.mybir as mybir
from concourse.bass_utils import run_bass_kernel_spmd
from concourse.alu_op_type import AluOpType

F32 = mybir.dt.float32
F32R = mybir.dt.float32r
F16 = mybir.dt.float16
BF16 = mybir.dt.bfloat16
AF = mybir.ActivationFunctionType
AX = mybir.AxisListType
OP = AluOpType

S = 2048
D = 2048
NT = 16
NH = 32
NG = 8
DH = 64
DI = 1024
NE = 16
TOPK = 4
EPS = 1e-5

ENGINES = ("tensor", "vector", "scalar", "sync", "gpsimd")
SLOPES = [float(np.float32(2.0 ** (-(h + 1) * 8.0 / 32.0))) for h in range(NH)]


class Prog:
    def __init__(self, nc, block):
        self.nc = nc
        self.block = block
        self.ops = {e: [] for e in ENGINES}
        self.cnt = {}
        self.sems = {}

    def mksem(self, stack, name):
        self.sems[name] = stack.enter_context(self.nc.semaphore(name))
        self.cnt[name] = 0

    def emit(self, eng, fn=None, waits=(), inc=None, incn=1):
        w = tuple(waits)
        if inc is not None:
            self.cnt[inc] += incn
        sems = self.sems

        def run(e, fn=fn, w=w, inc=inc, incn=incn, sems=sems):
            for sname, v in w:
                e.wait_ge(sems[sname], v)
            if fn is not None:
                inst = fn(e)
                if inc is not None:
                    inst.then_inc(sems[inc], incn)
            elif inc is not None:
                e.nop().then_inc(sems[inc], incn)

        self.ops[eng].append(run)
        return self.cnt[inc] if inc is not None else None

    def flush(self):
        for e in ENGINES:
            pend = self.ops[e]
            if pend:
                self.ops[e] = []
                getattr(self.block, e)(lambda eng, pend=pend: [f(eng) for f in pend])

    def barrier(self):
        tgt = self.cnt["sB"] + len(ENGINES)
        for e in ENGINES:
            self.emit(e, None, inc="sB")
        for e in ENGINES:
            self.emit(e, None, waits=[("sB", tgt)])


def build(dev=None):
    nc = bass.Bass("TRN2", target_bir_lowering=False)

    x_h = nc.dram_tensor("x", [S, D], F32, kind="ExternalInput").ap()
    wq_h = nc.dram_tensor("wq", [D, D], F32, kind="ExternalInput").ap()
    wk_h = nc.dram_tensor("wk", [D, 512], F32, kind="ExternalInput").ap()
    wv_h = nc.dram_tensor("wv", [D, 512], F32, kind="ExternalInput").ap()
    wo_h = nc.dram_tensor("wo", [D, D], F32, kind="ExternalInput").ap()
    bo_h = nc.dram_tensor("bo", [1, D], F32, kind="ExternalInput").ap()
    gwt_h = nc.dram_tensor("gwt", [D, NE], F32, kind="ExternalInput").ap()
    eb_h = nc.dram_tensor("ebias", [1, NE], F32, kind="ExternalInput").ap()
    wgt_h = nc.dram_tensor("wgT", [NE, 8, 128, 16 * 128], F32, kind="ExternalInput").ap()
    wut_h = nc.dram_tensor("wuT", [NE, 8, 128, 16 * 128], F32, kind="ExternalInput").ap()
    wdt_h = nc.dram_tensor("wdT", [NE, 8, 128, D], F32, kind="ExternalInput").ap()
    swgt_h = nc.dram_tensor("swgT", [8, 128, 16 * 128], F32, kind="ExternalInput").ap()
    swut_h = nc.dram_tensor("swuT", [8, 128, 16 * 128], F32, kind="ExternalInput").ap()
    swdt_h = nc.dram_tensor("swdT", [8, 128, D], F32, kind="ExternalInput").ap()
    d1_h = nc.dram_tensor("d1t", [128, 16 * 512], F16, kind="ExternalInput").ap()
    ropec_h = nc.dram_tensor("ropec", [128, 4 * 16 * 32], F32, kind="ExternalInput").ap()
    identb_h = nc.dram_tensor("identb", [128, 128], BF16, kind="ExternalInput").ap()
    identf_h = nc.dram_tensor("identf", [128, 128], F32, kind="ExternalInput").ap()
    triu_h = nc.dram_tensor("triu", [128, 128], BF16, kind="ExternalInput").ap()
    lowm_h = nc.dram_tensor("lowm", [128, NE * NE], F32, kind="ExternalInput").ap()

    out_h = nc.dram_tensor("out", [S, D], F32, kind="ExternalOutput").ap()
    qr_h = nc.dram_tensor("qr_s", [S, D], BF16,
                          kind="ExternalOutput" if dev == "p0" else "Internal").ap()
    x1_h = nc.dram_tensor("x1_s", [S, D], F32,
                          kind="ExternalOutput" if dev == "p2" else "Internal").ap()
    taps = {}
    if dev == "p0":
        taps["krT"] = nc.dram_tensor("t_krT", [4 * 128, S], BF16, kind="ExternalOutput").ap()
        taps["vext"] = nc.dram_tensor("t_vext", [128, NT * 520], BF16, kind="ExternalOutput").ap()
    if dev and dev.startswith("p1"):
        taps["ctxT"] = nc.dram_tensor("t_ctxT", [128, 16 * S], BF16, kind="ExternalOutput").ap()
    if dev == "p2":
        taps["m2"] = nc.dram_tensor("t_m2", [128, NT], F32, kind="ExternalOutput").ap()
        taps["r2"] = nc.dram_tensor("t_r2", [128, NT], F32, kind="ExternalOutput").ap()
    if dev == "p3r":
        taps["cw"] = nc.dram_tensor("t_cw", [2 * 128, 8 * NE], F32, kind="ExternalOutput").ap()
    if dev and dev.startswith("p3e"):
        pass

    st = contextlib.ExitStack()
    with st:
        en = st.enter_context
        block = en(nc.Block())
        p = Prog(nc, block)
        for sname in ("sW", "sX", "sO", "sT", "sV", "sS", "sVs", "sSs", "sB"):
            p.mksem(st, sname)

        identb = en(nc.sbuf_tensor("c_identb", [128, 128], BF16))
        identf = en(nc.sbuf_tensor("c_identf", [128, 128], F32))
        triu = en(nc.sbuf_tensor("c_triu", [128, 128], BF16))
        onesb = en(nc.sbuf_tensor("c_onesb", [1, 128], BF16))
        onesf = en(nc.sbuf_tensor("c_onesf", [1, 128], F32))
        ones_r = en(nc.sbuf_tensor("c_ones_r", [1, 128], F32R))
        oneshi = en(nc.sbuf_tensor("c_oneshi", [1, 128], F32R))
        bm30 = en(nc.sbuf_tensor("c_bm30", [128, 1], F32))
        bo_bf = en(nc.sbuf_tensor("c_bo_bf", [1, D], BF16))
        gw_f = en(nc.sbuf_tensor("c_gw_f", [128, NE * NE], F32))
        gw_bf = en(nc.sbuf_tensor("c_gw_bf", [128, NE * NE], BF16))
        lowm = en(nc.sbuf_tensor("c_lowm", [128, NE * NE], F32))
        eb_f = en(nc.sbuf_tensor("c_eb_f", [1, NE], F32))
        eb_r = en(nc.sbuf_tensor("c_eb_r", [1, NE], F32R))
        ebb = en(nc.sbuf_tensor("c_ebb", [128, NE], F32))
        m2 = en(nc.sbuf_tensor("c_m2", [128, NT], F32))
        r2 = en(nc.sbuf_tensor("c_r2", [128, NT], F32))
        sums_acc = en(nc.sbuf_tensor("c_sumacc", [128, NT], F32))
        ssq_acc = en(nc.sbuf_tensor("c_ssqacc", [128, NT], F32))
        s1 = en(nc.sbuf_tensor("c_s1", [128, 1], F32))
        mean = en(nc.sbuf_tensor("c_mean", [128, 1], F32))
        ssqs = en(nc.sbuf_tensor("c_ssqs", [128, 1], F32))
        var = en(nc.sbuf_tensor("c_var", [128, 1], F32))
        sdv = en(nc.sbuf_tensor("c_sdv", [128, 1], F32))
        rstd = en(nc.sbuf_tensor("c_rstd", [128, 1], F32))
        sqdump = en(nc.sbuf_tensor("c_sqdump", [128, D], BF16))
        arena = en(nc.sbuf_tensor("c_arena", [128, 16 * S], BF16))
        arena3 = arena[:].rearrange("p (c t) -> p c t", c=16)
        bo_f = arena[0:1, 0:2 * D].bitcast(F32)
        gw3b = gw_bf[:].rearrange("p (k n) -> p k n", k=16)
        lowm3 = lowm[:].rearrange("p (a b) -> p a b", a=NE)

        pb = [en(nc.psum_tensor("pb%d" % i, [128, 512], F32)) for i in range(8)]

        def pbb(i):
            return pb[i][:].bitcast(BF16)

        tslot_ev = [0, 0]

        def emit_transpose(src_ap, dst_ap, ident, extra_wait=None):
            sl = emit_transpose.flip
            emit_transpose.flip ^= 1
            pt = pbb(6 + sl) if src_ap.dtype == BF16 else pb[6 + sl][:]
            np_, nf = src_ap.shape
            w_ = [("sV", tslot_ev[sl])]
            if extra_wait:
                w_.append(extra_wait)
            p.emit("tensor", lambda e, pt=pt, src_ap=src_ap, ident=ident, np_=np_, nf=nf: e.transpose(
                pt[0:nf, 0:np_], src_ap, ident), waits=w_, inc="sT")
            p.emit("vector", lambda e, pt=pt, dst_ap=dst_ap, np_=np_, nf=nf: e.tensor_copy(
                dst_ap, pt[0:nf, 0:np_]), waits=[("sT", p.cnt["sT"])], inc="sV")
            tslot_ev[sl] = p.cnt["sV"]
        emit_transpose.flip = 0

        for dst, src in ((identb[:], identb_h), (identf[:], identf_h), (triu[:], triu_h),
                         (bo_f, bo_h), (lowm[:], lowm_h), (eb_f[:], eb_h)):
            p.emit("sync", lambda e, d=dst, s=src: e.dma_start(d, s), inc="sW", incn=16)
        gwt3 = gwt_h.rearrange("(k p) n -> k p n", p=128)
        gwf3 = gw_f[:].rearrange("p (k n) -> p k n", k=16)
        for kc in range(16):
            p.emit("sync", lambda e, kc=kc: e.dma_start(gwf3[:, kc, :], gwt3[kc]), inc="sW", incn=16)
        wconst = p.cnt["sW"]
        p.emit("vector", lambda e: e.memset(onesb[:], 1.0))
        p.emit("vector", lambda e: e.memset(onesf[:], 1.0))
        p.emit("vector", lambda e: e.memset(bm30[:], -30.0), inc="sVs")
        p.emit("vector", lambda e: e.tensor_copy(ones_r[:], onesf[:]),
               waits=[("sVs", p.cnt["sVs"])], inc="sVs")
        p.emit("vector", lambda e: e.memset(onesf[:, 0:64], 0.0), inc="sVs")
        p.emit("vector", lambda e: e.tensor_copy(oneshi[:], onesf[:]),
               waits=[("sVs", p.cnt["sVs"])], inc="sVs")
        p.emit("vector", lambda e: e.tensor_copy(eb_r[:], eb_f[:]),
               waits=[("sW", wconst), ("sVs", p.cnt["sVs"])], inc="sV")
        p.emit("vector", lambda e: e.tensor_copy(bo_bf[:], bo_f), inc="sV")
        p.emit("vector", lambda e: e.tensor_copy(gw_bf[:], gw_f[:]), inc="sV")
        p.emit("vector", lambda e: e.memset(sums_acc[:], 0.0), inc="sV")
        p.emit("vector", lambda e: e.memset(ssq_acc[:], 0.0), inc="sV")
        vconst = p.cnt["sV"]
        p.emit("tensor", lambda e: e.matmul(pb[6][:, 0:NE], ones_r[:], eb_r[:],
                                            start=True, stop=True),
               waits=[("sV", vconst)], inc="sT")
        p.emit("vector", lambda e: e.tensor_copy(ebb[:], pb[6][:, 0:NE]),
               waits=[("sT", p.cnt["sT"])], inc="sV")
        tslot_ev[0] = p.cnt["sV"]
        p.flush()

        with contextlib.ExitStack() as att:
            ea = att.enter_context
            krt = [ea(nc.sbuf_tensor("krt%d" % i, [128, S], BF16)) for i in range(4)]
            vext = ea(nc.sbuf_tensor("vext", [128, NT * 520], BF16))
            vext3 = vext[:].rearrange("p (t c) -> p t c", t=NT)

            with contextlib.ExitStack() as p0s:
                e0 = p0s.enter_context
                ropec = e0(nc.sbuf_tensor("s_ropec", [128, 4 * 16 * 32], F32))
                ropec4 = ropec[:].rearrange("p (k t i) -> p k t i", k=4, t=16)
                xch = [e0(nc.sbuf_tensor("xch%d" % i, [128, D], F32)) for i in range(2)]
                xc = e0(nc.sbuf_tensor("xc", [128, D], F32))
                xn1c = e0(nc.sbuf_tensor("xn1c", [128, D], BF16))
                ws = [e0(nc.sbuf_tensor("ws%d" % i, [128, 512], F32)) for i in range(2)]
                wub = [e0(nc.sbuf_tensor("wub%d" % i, [128, 16 * 512], BF16)) for i in range(2)]
                rt = [e0(nc.sbuf_tensor("rt%d" % i, [128, 256], F32)) for i in range(8)]
                qrp = [e0(nc.sbuf_tensor("qrp%d" % i, [128, 512], BF16)) for i in range(2)]

                p.emit("sync", lambda e: e.dma_start(ropec[:], ropec_h), inc="sW", incn=16)
                vones = vext3.rearrange("p t (g u) -> p t g u", g=NG)[:, :, :, 64].opt()
                p.emit("vector", lambda e: e.memset(vones, 1.0), inc="sV")

                xc_done = [0] * NT
                prev_var = 0
                for tc in range(NT):
                    bx = xch[tc % 2]
                    w_ = [("sV", xc_done[tc - 2])] if tc >= 2 else []
                    p.emit("gpsimd", lambda e, bx=bx, tc=tc: e.dma_start(
                        bx[:], x_h[tc * 128:(tc + 1) * 128, :]), waits=w_, inc="sX", incn=16)
                    ldx = p.cnt["sX"]
                    p.emit("vector", lambda e, bx=bx: e.reduce_sum(s1[:], bx[:], axis=AX.X),
                           waits=[("sX", ldx), ("sS", p.cnt["sS"])], inc="sVs")
                    p.emit("vector", lambda e: e.tensor_scalar_mul(mean[:], s1[:], 1.0 / D),
                           waits=[("sVs", p.cnt["sVs"])], inc="sVs")
                    p.emit("vector", lambda e, bx=bx: e.tensor_scalar(
                        xc[:], bx[:], mean[:], None, op0=OP.subtract),
                        waits=[("sVs", p.cnt["sVs"])], inc="sV")
                    xc_done[tc] = p.cnt["sV"]
                    p.emit("scalar", lambda e: e.activation(
                        sqdump[:], xc[:], AF.Square, accum_out=ssqs[:]),
                        waits=[("sV", xc_done[tc]), ("sVs", prev_var)], inc="sS")
                    p.emit("vector", lambda e: e.tensor_scalar(
                        var[:], ssqs[:], 1.0 / D, EPS, op0=OP.mult, op1=OP.add),
                        waits=[("sS", p.cnt["sS"])], inc="sVs")
                    prev_var = p.cnt["sVs"]
                    p.emit("scalar", lambda e: e.activation(sdv[:], var[:], AF.Sqrt),
                           waits=[("sVs", p.cnt["sVs"])], inc="sS")
                    p.emit("vector", lambda e: e.reciprocal(rstd[:], sdv[:]),
                           waits=[("sS", p.cnt["sS"])], inc="sVs")
                    p.emit("vector", lambda e: e.tensor_scalar(
                        xn1c[:], xc[:], rstd[:], None, op0=OP.mult),
                        waits=[("sVs", p.cnt["sVs"])], inc="sV")
                    xnv = p.cnt["sV"]
                    for fc in range(16):
                        emit_transpose(xn1c[:, fc * 128:(fc + 1) * 128],
                                       arena3[:, fc, tc * 128:(tc + 1) * 128],
                                       identb[:], extra_wait=("sV", xnv))
                xn1_all = p.cnt["sV"]

                units = [("k", wk_h, 0), ("v", wv_h, 0), ("q", wq_h, 0),
                         ("q", wq_h, 1), ("q", wq_h, 2), ("q", wq_h, 3)]
                unit_mm_done = [0] * len(units)
                cast_done = [0] * len(units)
                ws_cast = [0, 0]
                bank_free = [0, 0]
                qrp_free = [("sV", 0), ("sV", 0)]
                for ui, (kind, wh, fo) in enumerate(units):
                    ub = wub[ui % 2]
                    ub3 = ub[:].rearrange("p (k n) -> p k n", k=16)
                    for kc in range(16):
                        wsb = ws[kc % 2]
                        p.emit("sync", lambda e, wsb=wsb, wh=wh, kc=kc, fo=fo: e.dma_start(
                            wsb[:], wh[kc * 128:(kc + 1) * 128, fo * 512:(fo + 1) * 512]),
                            waits=[("sV", ws_cast[kc % 2])], inc="sW", incn=16)
                        ldw = p.cnt["sW"]
                        w2 = [("sW", ldw)]
                        if ui >= 2 and kc == 0:
                            w2.append(("sT", unit_mm_done[ui - 2]))
                        p.emit("vector", lambda e, ub3=ub3, kc=kc, wsb=wsb: e.tensor_copy(
                            ub3[:, kc, :], wsb[:]), waits=w2, inc="sV")
                        ws_cast[kc % 2] = p.cnt["sV"]
                    cast_done[ui] = p.cnt["sV"]
                    for tc in range(NT):
                        bk = (ui * NT + tc) % 2
                        ps = pb[bk]
                        for kc in range(16):
                            w_ = ([("sV", max(cast_done[ui], xn1_all, bank_free[bk]))]
                                  if kc == 0 else [])
                            p.emit("tensor", lambda e, ps=ps, kc=kc, tc=tc, ub3=ub3: e.matmul(
                                ps[:], arena3[:, kc, tc * 128:(tc + 1) * 128], ub3[:, kc, :],
                                start=(kc == 0), stop=(kc == 15)),
                                waits=w_, inc="sT" if kc == 15 else None)
                        mmd = p.cnt["sT"]
                        if kind == "v":
                            vdst = vext3[:, tc, :].rearrange("p (g u) -> p g u", g=NG)[:, :, 0:64]
                            psrc = ps[:].rearrange("p (g u) -> p g u", g=NG)
                            p.emit("vector", lambda e, vdst=vdst, psrc=psrc: e.tensor_copy(
                                vdst, psrc), waits=[("sT", mmd)], inc="sV")
                            bank_free[bk] = p.cnt["sV"]
                        else:
                            ti = 0 if kind == "q" else 2
                            cq = ropec4[:, ti, tc, :].unsqueeze(1).broadcast_to([128, 8, 32])
                            sq = ropec4[:, ti + 1, tc, :].unsqueeze(1).broadcast_to([128, 8, 32])
                            p4 = ps[:].rearrange("p (h i two) -> p h i two", h=8, two=2)
                            pev = p4[:, :, :, 0].opt()
                            pod = p4[:, :, :, 1].opt()
                            rb = (ui * NT + tc) % 2
                            t1, t2, t3, t4 = (rt[4 * rb + i][:].rearrange(
                                "p (h i) -> p h i", h=8) for i in range(4))
                            qb = qrp[rb]
                            q4 = qb[:].rearrange("p (h i two) -> p h i two", h=8, two=2)
                            qev = q4[:, :, :, 0].opt()
                            qod = q4[:, :, :, 1].opt()
                            wqr = [("sT", mmd), qrp_free[rb]]
                            p.emit("vector", lambda e, t1=t1, pev=pev, cq=cq: e.tensor_tensor(
                                t1, pev, cq, op=OP.mult), waits=wqr, inc="sV")
                            p.emit("vector", lambda e, t2=t2, pod=pod, sq=sq: e.tensor_tensor(
                                t2, pod, sq, op=OP.mult), inc="sV")
                            p.emit("vector", lambda e, t3=t3, pev=pev, sq=sq: e.tensor_tensor(
                                t3, pev, sq, op=OP.mult), inc="sV")
                            p.emit("vector", lambda e, t4=t4, pod=pod, cq=cq: e.tensor_tensor(
                                t4, pod, cq, op=OP.mult), inc="sV")
                            bank_free[bk] = p.cnt["sV"]
                            p.emit("vector", lambda e, qev=qev, t1=t1, t2=t2: e.tensor_tensor(
                                qev, t1, t2, op=OP.subtract), inc="sV")
                            p.emit("vector", lambda e, qod=qod, t3=t3, t4=t4: e.tensor_tensor(
                                qod, t3, t4, op=OP.add), inc="sV")
                            rpv = p.cnt["sV"]
                            if kind == "q":
                                p.emit("gpsimd", lambda e, qb=qb, tc=tc, fo=fo: e.dma_start(
                                    qr_h[tc * 128:(tc + 1) * 128, fo * 512:(fo + 1) * 512],
                                    qb[:]), waits=[("sV", rpv)], inc="sO", incn=16)
                                qrp_free[rb] = ("sO", p.cnt["sO"])
                            else:
                                for fb in range(4):
                                    emit_transpose(qb[:, fb * 128:(fb + 1) * 128],
                                                   krt[fb][:, tc * 128:(tc + 1) * 128],
                                                   identb[:], extra_wait=("sV", rpv))
                                qrp_free[rb] = ("sV", p.cnt["sV"])
                        unit_mm_done[ui] = p.cnt["sT"]
                qr_spill_done = p.cnt["sO"]

                if dev == "p0":
                    wfin = [("sV", p.cnt["sV"]), ("sT", p.cnt["sT"]), ("sO", qr_spill_done)]
                    for i in range(4):
                        p.emit("gpsimd", lambda e, i=i: e.dma_start(
                            taps["krT"][i * 128:(i + 1) * 128, :], krt[i][:]),
                            waits=wfin if i == 0 else [], inc="sO", incn=16)
                    p.emit("gpsimd", lambda e: e.dma_start(taps["vext"], vext[:]),
                           inc="sO", incn=16)
                    p.emit("gpsimd", None, waits=[("sO", p.cnt["sO"])])
                    p.flush()
                    return nc
                p.barrier()
                p.flush()

            with contextlib.ExitStack() as p1s:
                e1 = p1s.enter_context
                d1t = e1(nc.sbuf_tensor("s_d1t", [128, 16 * 512], F16))
                d1t3 = d1t[:].rearrange("p (a b) -> p a b", a=16)
                ub_ = [e1(nc.sbuf_tensor("ub%d" % i, [128, S], F32)) for i in range(2)]
                pbf = [e1(nc.sbuf_tensor("pbf%d" % i, [128, S], BF16)) for i in range(2)]
                recT = e1(nc.sbuf_tensor("recT", [128, 512], F32))
                recR = e1(nc.sbuf_tensor("recR", [32, S], F32R))
                recrow_t = e1(nc.sbuf_tensor("recrow", [1, S], F32R))
                sumstg = e1(nc.sbuf_tensor("sumstg", [128, S], F32))
                sums = sumstg
                ostg = [e1(nc.sbuf_tensor("ostg%d" % i, [128, 512], BF16)) for i in range(2)]
                recrow = recrow_t[:]

                qhead = [e1(nc.sbuf_tensor("qhead%d" % i, [64, S], BF16)) for i in range(2)]
                khead = [e1(nc.sbuf_tensor("khead%d" % i, [64, S], BF16)) for i in range(2)]
                p.emit("sync", lambda e: e.dma_start(d1t[:], d1_h), inc="sW", incn=16)
                d1_ld = p.cnt["sW"]

                with contextlib.ExitStack() as pqs:
                    qrc = [pqs.enter_context(nc.sbuf_tensor("qrc%d" % i, [128, S], BF16))
                           for i in range(2)]
                    qrc_free = [0, 0]
                    for tc in range(NT):
                        qb = qrc[tc % 2]
                        w_ = [("sV", qrc_free[tc % 2])]
                        if tc == 0:
                            w_.append(("sO", qr_spill_done))
                        p.emit("gpsimd", lambda e, qb=qb, tc=tc: e.dma_start(
                            qb[:], qr_h[tc * 128:(tc + 1) * 128, :]), waits=w_,
                            inc="sX", incn=16)
                        ldq = p.cnt["sX"]
                        for fc in range(16):
                            emit_transpose(qb[:, fc * 128:(fc + 1) * 128],
                                           arena3[:, fc, tc * 128:(tc + 1) * 128],
                                           identb[:], extra_wait=("sX", ldq))
                        qrc_free[tc % 2] = p.cnt["sV"]
                    qrT_all = p.cnt["sV"]
                    p.flush()
                if dev == "p1qr":
                    p.emit("gpsimd", lambda e: e.dma_start(taps["ctxT"], arena[:]),
                           waits=[("sV", p.cnt["sV"])], inc="sO", incn=16)
                    p.emit("gpsimd", None, waits=[("sO", p.cnt["sO"])])
                    p.flush()
                    return nc

                sbank_free = [0, 0]
                ub_free = [0, 0]
                pbuf_free = [0, 0]
                cb_ev = [0] * 4
                ctx_stop = [0] * 4
                sct = 0
                qh_used = [0, 0]
                kh_used = [0, 0]
                stg_used = [("sV", 0), ("sV", 0)]
                sumstg_used = 0
                sums_dma = 0
                nheads = NH
                if dev and dev.startswith("p1h"):
                    nheads = int(dev[3:])
                for h in range(nheads):
                    g = h // 4
                    slope = SLOPES[h]
                    fcq, rq = h // 2, (h % 2) * 64
                    hb = h % 2
                    p.emit("gpsimd", lambda e, hb=hb, rq=rq, fcq=fcq: e.dma_start(
                        qhead[hb][:], arena3[rq:rq + 64, fcq, :]),
                        waits=[("sV", qrT_all), ("sT", qh_used[hb])], inc="sX", incn=16)
                    qh_ld = p.cnt["sX"]
                    kb = (h // 4) % 2
                    if h % 4 == 0:
                        rk = (g % 2) * 64
                        p.emit("gpsimd", lambda e, kb=kb, g=g, rk=rk: e.dma_start(
                            khead[kb][:], krt[g // 2][rk:rk + 64, :]),
                            waits=[("sT", kh_used[kb])], inc="sX", incn=16)
                    kh_ld = p.cnt["sX"]
                    for kc in range(16):
                        j0 = kc // 4
                        qs0 = kc * 128
                        par = kc % 2
                        for j in range(j0, 4):
                            c0 = max(qs0, j * 512)
                            w = (j + 1) * 512 - c0
                            bk = sct % 2
                            sct += 1
                            wts = [("sV", sbank_free[bk]), ("sX", max(qh_ld, kh_ld))]
                            p.emit("tensor", lambda e, bk=bk, w=w, kc=kc, c0=c0, hb=hb, kb=kb: e.matmul(
                                pb[bk][:, 0:w],
                                khead[kb][:, kc * 128:(kc + 1) * 128],
                                qhead[hb][:, c0:c0 + w],
                                start=True, stop=True), waits=wts, inc="sT")
                            mmv = p.cnt["sT"]
                            di = (kc * 128 - j * 512 + 1536) // 128
                            off = c0 - j * 512
                            wts = [("sT", mmv), ("sS", ub_free[par])]
                            if kc <= 1 and h == 0:
                                wts.append(("sW", d1_ld))
                            p.emit("vector", lambda e, par=par, c0=c0, w=w, di=di, off=off,
                                   slope=slope, bk=bk: e.scalar_tensor_tensor(
                                ub_[par][:, c0:c0 + w], d1t3[:, di, off:off + w], slope,
                                pb[bk][:, 0:w], op0=OP.mult, op1=OP.add),
                                waits=wts, inc="sV")
                            sbank_free[bk] = p.cnt["sV"]
                        qh_used[hb] = p.cnt["sT"]
                        kh_used[kb] = p.cnt["sT"]
                        sttv = p.cnt["sV"]
                        p.emit("scalar", lambda e, par=par, qs0=qs0: e.activation(
                            ub_[par][:, qs0:S], ub_[par][:, qs0:S], AF.Tanh, scale=1.0 / 30.0),
                            waits=[("sV", sttv), ("sT", pbuf_free[par])])
                        p.emit("scalar", lambda e, par=par, qs0=qs0: e.activation(
                            pbf[par][:, qs0:S], ub_[par][:, qs0:S], AF.Exp,
                            bias=bm30[:], scale=30.0), inc="sS")
                        ub_free[par] = p.cnt["sS"]
                        expv = p.cnt["sS"]
                        p.emit("vector", lambda e, par=par, qs0=qs0: e.tensor_tensor(
                            pbf[par][:, qs0:qs0 + 128], pbf[par][:, qs0:qs0 + 128],
                            triu[:], op=OP.mult), waits=[("sS", expv)], inc="sV")
                        maskv = p.cnt["sV"]
                        for j in range(j0, 4):
                            c0 = max(qs0, j * 512)
                            w = (j + 1) * 512 - c0
                            kl = min(15, 4 * j + 3)
                            wts = [("sV", maskv)]
                            if kc == 0:
                                wts.append(("sV", cb_ev[j]))
                                wts.append(("sO", sums_dma))
                            p.emit("tensor", lambda e, j=j, c0=c0, w=w, kc=kc, par=par, g=g: e.matmul(
                                pb[2 + j][0:65, c0 - j * 512:c0 - j * 512 + w],
                                vext3[:, kc, g * 65:g * 65 + 65],
                                pbf[par][:, c0:c0 + w],
                                start=(kc == 0), stop=(kc == kl)), waits=wts, inc="sT")
                            if kc == kl:
                                ctx_stop[j] = p.cnt["sT"]
                        pbuf_free[par] = p.cnt["sT"]
                    for j in range(4):
                        if rq == 0:
                            p.emit("vector", lambda e, j=j, fcq=fcq: e.tensor_copy(
                                arena3[0:64, fcq, j * 512:(j + 1) * 512], pb[2 + j][0:64, :]),
                                waits=[("sT", ctx_stop[j])], inc="sV")
                        else:
                            sg_ = j % 2
                            p.emit("vector", lambda e, j=j, sg_=sg_: e.tensor_copy(
                                ostg[sg_][:] if False else ostg[sg_][0:64, :], pb[2 + j][0:64, :]),
                                waits=[("sT", ctx_stop[j]), stg_used[sg_]], inc="sV")
                            p.emit("gpsimd", lambda e, j=j, fcq=fcq, sg_=sg_: e.dma_start(
                                arena3[64:128, fcq, j * 512:(j + 1) * 512], ostg[sg_][0:64, :]),
                                waits=[("sV", p.cnt["sV"])], inc="sX", incn=16)
                            stg_used[sg_] = ("sX", p.cnt["sX"])
                        p.emit("vector", lambda e, j=j: e.tensor_copy(
                            sumstg[64:65, j * 512:(j + 1) * 512], pb[2 + j][64:65, :]),
                            waits=([("sT", ctx_stop[j])] if rq == 0 else []) +
                                  ([("sO", sums_dma)] if j == 0 else []), inc="sV")
                        cb_ev[j] = p.cnt["sV"]
                    p.emit("gpsimd", lambda e, h=h: e.dma_start(
                        sums[h:h + 1, :].bitcast(F32) if False else sums[h:h + 1, :],
                        sumstg[64:65, :]), waits=[("sV", cb_ev[3])], inc="sO", incn=16)
                    sums_dma = p.cnt["sO"]
                ctx_all_ev = max(p.cnt["sV"], 0)
                stg_dma_all = p.cnt["sX"]
                if dev and dev.startswith("p1h"):
                    p.emit("gpsimd", lambda e: e.dma_start(taps["ctxT"], arena[:]),
                           waits=[("sV", p.cnt["sV"]), ("sX", stg_dma_all)], inc="sO", incn=16)
                    p.emit("gpsimd", None, waits=[("sO", p.cnt["sO"])])
                    p.flush()
                    return nc

                for tc in range(NT):
                    sl = emit_transpose.flip
                    emit_transpose.flip ^= 1
                    pt = pb[6 + sl]
                    p.emit("tensor", lambda e, pt=pt, tc=tc: e.transpose(
                        pt[0:128, 0:32], sums[0:32, tc * 128:(tc + 1) * 128],
                        identf[0:32, 0:32]),
                        waits=[("sV", tslot_ev[sl]), ("sO", sums_dma)], inc="sT")
                    p.emit("vector", lambda e, pt=pt, tc=tc: e.reciprocal(
                        recT[:, tc * 32:(tc + 1) * 32], pt[0:128, 0:32]),
                        waits=[("sT", p.cnt["sT"])], inc="sV")
                    tslot_ev[sl] = p.cnt["sV"]
                recT_done = p.cnt["sV"]
                for tc in range(NT):
                    sl = emit_transpose.flip
                    emit_transpose.flip ^= 1
                    pt = pb[6 + sl]
                    p.emit("tensor", lambda e, pt=pt, tc=tc: e.transpose(
                        pt[0:32, 0:128], recT[:, tc * 32:(tc + 1) * 32], identf[:]),
                        waits=[("sV", max(tslot_ev[sl], recT_done))], inc="sT")
                    p.emit("vector", lambda e, pt=pt, tc=tc: e.tensor_copy(
                        recR[0:32, tc * 128:(tc + 1) * 128], pt[0:32, 0:128]),
                        waits=[("sT", p.cnt["sT"])], inc="sV")
                    tslot_ev[sl] = p.cnt["sV"]
                recR_done = p.cnt["sV"]
                rr_dma = 0
                for h in range(NH):
                    fcq, rq = h // 2, (h % 2) * 64
                    p.emit("gpsimd", lambda e, h=h: e.dma_start(
                        recrow, recR[h:h + 1, :]),
                        waits=[("sV", max(recR_done, p.cnt["sV"])), ("sX", stg_dma_all)],
                        inc="sX", incn=16)
                    rr_dma = p.cnt["sX"]
                    for j in range(4):
                        sl = emit_transpose.flip
                        emit_transpose.flip ^= 1
                        pt = pb[6 + sl]
                        lt = ones_r[:, 0:64] if rq == 0 else oneshi[:]
                        mrows = 64 if rq == 0 else 128
                        p.emit("tensor", lambda e, pt=pt, lt=lt, mrows=mrows, j=j: e.matmul(
                            pt[0:mrows, :], lt, recrow[0:1, j * 512:(j + 1) * 512],
                            start=True, stop=True),
                            waits=[("sV", tslot_ev[sl]), ("sX", rr_dma)], inc="sT")
                        p.emit("vector", lambda e, pt=pt, rq=rq, fcq=fcq, j=j: e.tensor_tensor(
                            arena3[rq:rq + 64, fcq, j * 512:(j + 1) * 512],
                            arena3[rq:rq + 64, fcq, j * 512:(j + 1) * 512],
                            pt[rq:rq + 64, :], op=OP.mult),
                            waits=[("sT", p.cnt["sT"])], inc="sV")
                        tslot_ev[sl] = p.cnt["sV"]

                if dev == "p1":
                    p.emit("gpsimd", lambda e: e.dma_start(taps["ctxT"], arena[:]),
                           waits=[("sV", p.cnt["sV"])], inc="sO", incn=16)
                    p.emit("gpsimd", None, waits=[("sO", p.cnt["sO"])])
                    p.flush()
                    return nc
                p.barrier()
                p.flush()

        with contextlib.ExitStack() as p2s:
            e2 = p2s.enter_context
            ws = [e2(nc.sbuf_tensor("w2s%d" % i, [128, 512], F32)) for i in range(2)]
            wub = [e2(nc.sbuf_tensor("w2ub%d" % i, [128, 16 * 512], BF16)) for i in range(2)]
            xsl = [e2(nc.sbuf_tensor("xsl%d" % i, [128, 512], F32)) for i in range(2)]
            x1p = [e2(nc.sbuf_tensor("x1p%d" % i, [128, 512], F32)) for i in range(2)]
            ws_cast = [0, 0]
            unit_mm_done = [0] * 4
            bank_ev = [0, 0]
            x1_free = [0, 0]
            xsl_used = [0, 0]
            for fo in range(4):
                ub = wub[fo % 2]
                ub3 = ub[:].rearrange("p (k n) -> p k n", k=16)
                for kc in range(16):
                    wsb = ws[kc % 2]
                    p.emit("sync", lambda e, wsb=wsb, kc=kc, fo=fo: e.dma_start(
                        wsb[:], wo_h[kc * 128:(kc + 1) * 128, fo * 512:(fo + 1) * 512]),
                        waits=[("sV", ws_cast[kc % 2])], inc="sW", incn=16)
                    ldw = p.cnt["sW"]
                    w2 = [("sW", ldw)]
                    if fo >= 2 and kc == 0:
                        w2.append(("sT", unit_mm_done[fo - 2]))
                    p.emit("vector", lambda e, ub3=ub3, kc=kc, wsb=wsb: e.tensor_copy(
                        ub3[:, kc, :], wsb[:]), waits=w2, inc="sV")
                    ws_cast[kc % 2] = p.cnt["sV"]
                castd = p.cnt["sV"]
                for tc in range(NT):
                    bk = (fo * NT + tc) % 2
                    ps = pb[bk]
                    p.emit("gpsimd", lambda e, bk=bk, tc=tc, fo=fo: e.dma_start(
                        xsl[bk][:], x_h[tc * 128:(tc + 1) * 128, fo * 512:(fo + 1) * 512]),
                        waits=[("sV", xsl_used[bk])], inc="sX", incn=16)
                    ldx = p.cnt["sX"]
                    for kc in range(16):
                        w_ = [("sV", max(castd, bank_ev[bk]))] if kc == 0 else []
                        p.emit("tensor", lambda e, ps=ps, kc=kc, tc=tc, ub3=ub3: e.matmul(
                            ps[:], arena3[:, kc, tc * 128:(tc + 1) * 128], ub3[:, kc, :],
                            start=(kc == 0), stop=False), waits=w_)
                    p.emit("tensor", lambda e, ps=ps, fo=fo: e.matmul(
                        ps[:], onesb[:], bo_bf[:, fo * 512:(fo + 1) * 512],
                        start=False, stop=True), inc="sT")
                    mmd = p.cnt["sT"]
                    p.emit("vector", lambda e, ps=ps, bk=bk: e.tensor_tensor(
                        x1p[bk][:], ps[:], xsl[bk][:], op=OP.add),
                        waits=[("sT", mmd), ("sX", ldx), ("sO", x1_free[bk]),
                               ("sS", p.cnt["sS"])], inc="sV")
                    bank_ev[bk] = p.cnt["sV"]
                    xsl_used[bk] = p.cnt["sV"]
                    x1v = p.cnt["sV"]
                    p.emit("vector", lambda e, bk=bk: e.reduce_sum(
                        s1[:], x1p[bk][:], axis=AX.X), inc="sVs")
                    p.emit("vector", lambda e, tc=tc: e.tensor_tensor(
                        sums_acc[:, tc:tc + 1], sums_acc[:, tc:tc + 1], s1[:], op=OP.add),
                        waits=[("sVs", p.cnt["sVs"])], inc="sVs")
                    p.emit("scalar", lambda e, bk=bk: e.activation(
                        sqdump[:, 0:512], x1p[bk][:], AF.Square, accum_out=ssqs[:]),
                        waits=[("sV", x1v), ("sVs", p.cnt["sVs"])], inc="sS")
                    p.emit("vector", lambda e, tc=tc: e.tensor_tensor(
                        ssq_acc[:, tc:tc + 1], ssq_acc[:, tc:tc + 1], ssqs[:], op=OP.add),
                        waits=[("sS", p.cnt["sS"]), ("sVs", p.cnt["sVs"])], inc="sVs")
                    p.emit("gpsimd", lambda e, bk=bk, tc=tc, fo=fo: e.dma_start(
                        x1_h[tc * 128:(tc + 1) * 128, fo * 512:(fo + 1) * 512], x1p[bk][:]),
                        waits=[("sV", x1v)], inc="sO", incn=16)
                    x1_free[bk] = p.cnt["sO"]
                unit_mm_done[fo] = p.cnt["sT"]
            x1_all = p.cnt["sO"]
            sall = p.cnt["sVs"]
            p.emit("vector", lambda e: e.tensor_scalar_mul(m2[:], sums_acc[:], 1.0 / D),
                   waits=[("sVs", sall)], inc="sVs")
            p.emit("vector", lambda e: e.tensor_scalar_mul(r2[:], ssq_acc[:], 1.0 / D),
                   waits=[("sVs", p.cnt["sVs"])], inc="sVs")
            p.emit("vector", lambda e: e.tensor_tensor(
                sums_acc[:], m2[:], m2[:], op=OP.mult),
                waits=[("sVs", p.cnt["sVs"])], inc="sVs")
            p.emit("vector", lambda e: e.tensor_tensor(
                r2[:], r2[:], sums_acc[:], op=OP.subtract),
                waits=[("sVs", p.cnt["sVs"])], inc="sVs")
            p.emit("vector", lambda e: e.tensor_scalar_add(r2[:], r2[:], EPS),
                   waits=[("sVs", p.cnt["sVs"])], inc="sVs")
            p.emit("scalar", lambda e: e.activation(ssq_acc[:], r2[:], AF.Sqrt),
                   waits=[("sVs", p.cnt["sVs"])], inc="sS")
            p.emit("vector", lambda e: e.reciprocal(r2[:], ssq_acc[:]),
                   waits=[("sS", p.cnt["sS"])], inc="sVs")
            p.emit("vector", None, waits=[("sVs", p.cnt["sVs"])], inc="sV")
            r2_done = p.cnt["sV"]
            if dev == "p2":
                p.emit("gpsimd", lambda e: e.dma_start(taps["m2"], m2[:]),
                       waits=[("sV", r2_done), ("sO", x1_all)], inc="sO", incn=16)
                p.emit("gpsimd", lambda e: e.dma_start(taps["r2"], r2[:]), inc="sO", incn=16)
                p.emit("gpsimd", None, waits=[("sO", p.cnt["sO"])])
                p.flush()
                return nc
            p.barrier()
            p.flush()

        with contextlib.ExitStack() as p3s:
            e3 = p3s.enter_context
            out_acc = e3(nc.sbuf_tensor("out_acc", [128, 8 * D], F32))
            oa3 = out_acc[:].rearrange("p (c t) -> p c t", c=8)
            p3wd = e3(nc.sbuf_tensor("p3wd", [128, 8 * D], BF16))
            wd3 = p3wd[:].rearrange("p (c t) -> p c t", c=8)
            ws = [e3(nc.sbuf_tensor("w3s%d" % i, [128, 1024], F32)) for i in range(2)]
            sg = [e3(nc.sbuf_tensor("sg%d" % i, [128, 512], BF16)) for i in range(2)]
            t1b = e3(nc.sbuf_tensor("t1b", [128, 512], F32))
            cw_b = e3(nc.sbuf_tensor("cw_b", [128, 1024], F32))
            cwT = e3(nc.sbuf_tensor("cwT", [16, 1024], F32))
            cwT_r = e3(nc.sbuf_tensor("cwT_r", [16, 1024], F32R))
            cwrow = e3(nc.sbuf_tensor("cwrow", [1, 1024], F32R))
            cwh = e3(nc.sbuf_tensor("cwh", [128, 8 * NE], F32))
            rscr = e3(nc.sbuf_tensor("rscr", [128, 1024], F32))
            gtall = rscr[:, 0:256].rearrange("p (a b) -> p a b", a=NE)
            eqall = rscr[:, 256:512].rearrange("p (a b) -> p a b", a=NE)
            beats = rscr[:, 512:768].rearrange("p (a b) -> p a b", a=NE)
            scoresb = rscr[:, 768:784]
            selb = rscr[:, 784:800]
            rankb = rscr[:, 800:816]
            maskb = rscr[:, 816:832]
            wraw = rscr[:, 832:848]
            rsum = rscr[:, 848:849]
            rinv = rscr[:, 849:850]
            xn2T = arena[:, 0:16 * 1024].rearrange("p (c t) -> p c t", c=16)
            hT = arena[:, 16 * 1024:24 * 1024].rearrange("p (c t) -> p c t", c=8)
            wgb = [arena[:, 24 * 1024 + i * 2048: 24 * 1024 + (i + 1) * 2048]
                   .rearrange("p (k n) -> p k n", k=16) for i in range(2)]

            out_dma_prev = 0
            dpsum_free = [0, 0]
            for hf in range(2):
                base = hf * 1024
                ldh = []
                for c in range(8):
                    p.emit("gpsimd", lambda e, c=c, base=base: e.dma_start(
                        oa3[:, c, :], x1_h[base + c * 128: base + (c + 1) * 128, :]),
                        waits=([("sO", x1_all), ("sO", out_dma_prev)] if c == 0 else []),
                        inc="sX", incn=16)
                    ldh.append(p.cnt["sX"])
                xn2_done = 0
                for c in range(8):
                    gcol = hf * 8 + c
                    w_ = [("sX", ldh[c]), ("sT", xn2_done), ("sV", r2_done)]
                    p.emit("vector", lambda e, c=c, gcol=gcol: e.tensor_scalar(
                        sqdump[:], oa3[:, c, :], m2[:, gcol:gcol + 1], r2[:, gcol:gcol + 1],
                        op0=OP.subtract, op1=OP.mult), waits=w_, inc="sV")
                    xnv = p.cnt["sV"]
                    for fc in range(16):
                        emit_transpose(sqdump[:, fc * 128:(fc + 1) * 128],
                                       xn2T[:, fc, c * 128:(c + 1) * 128],
                                       identb[:], extra_wait=("sV", xnv))
                    xn2_done = p.cnt["sT"]
                    xn2v = p.cnt["sV"]
                    for kc in range(16):
                        w_ = [("sV", xn2v), ("sS", p.cnt["sS"])] if kc == 0 else []
                        p.emit("tensor", lambda e, kc=kc, c=c: e.matmul(
                            pb[5][:, 0:NE], xn2T[:, kc, c * 128:(c + 1) * 128],
                            gw3b[:, kc, :], start=(kc == 0), stop=(kc == 15)),
                            waits=w_, inc="sT" if kc == 15 else None)
                    lgd = p.cnt["sT"]
                    p.emit("scalar", lambda e: e.activation(scoresb, pb[5][:, 0:NE], AF.Sigmoid),
                           waits=[("sT", lgd), ("sVs", p.cnt["sVs"])], inc="sS")
                    p.emit("vector", lambda e: e.tensor_tensor(selb, scoresb, ebb[:], op=OP.add),
                           waits=[("sS", p.cnt["sS"])], inc="sVs")
                    selv = p.cnt["sVs"]
                    for ee in range(NE):
                        p.emit("vector", lambda e, ee=ee: e.tensor_scalar(
                            gtall[:, ee, :], selb, selb[:, ee:ee + 1], None, op0=OP.is_gt),
                            waits=[("sVs", selv)] if ee == 0 else [])
                        p.emit("vector", lambda e, ee=ee: e.tensor_scalar(
                            eqall[:, ee, :], selb, selb[:, ee:ee + 1], None, op0=OP.is_equal),
                            inc="sVs" if ee == NE - 1 else None)
                    p.emit("vector", lambda e: e.tensor_tensor(eqall, eqall, lowm3, op=OP.mult),
                           waits=[("sVs", p.cnt["sVs"])], inc="sVs")
                    p.emit("vector", lambda e: e.tensor_tensor(beats, gtall, eqall, op=OP.add),
                           waits=[("sVs", p.cnt["sVs"])], inc="sVs")
                    p.emit("vector", lambda e: e.reduce_sum(
                        rankb.rearrange("p (a o) -> p a o", o=1), beats, axis=AX.X),
                        waits=[("sVs", p.cnt["sVs"])], inc="sVs")
                    p.emit("vector", lambda e: e.tensor_single_scalar(
                        maskb, rankb, float(TOPK), op=OP.is_lt),
                        waits=[("sVs", p.cnt["sVs"])], inc="sVs")
                    p.emit("vector", lambda e: e.tensor_tensor(wraw, maskb, scoresb, op=OP.mult),
                           waits=[("sVs", p.cnt["sVs"])], inc="sVs")
                    p.emit("vector", lambda e: e.reduce_sum(rsum, wraw, axis=AX.X),
                           waits=[("sVs", p.cnt["sVs"])], inc="sVs")
                    p.emit("vector", lambda e: e.reciprocal(rinv, rsum),
                           waits=[("sVs", p.cnt["sVs"])], inc="sVs")
                    p.emit("vector", lambda e, c=c: e.tensor_scalar(
                        cwh[:, c * NE:(c + 1) * NE], wraw, rinv, None, op0=OP.mult),
                        waits=[("sVs", p.cnt["sVs"])], inc="sVs")
                    cwv = p.cnt["sVs"]
                    emit_transpose(cwh[:, c * NE:(c + 1) * NE],
                                   cwT[:, c * 128:(c + 1) * 128], identf[:],
                                   extra_wait=("sVs", cwv))
                p.emit("vector", lambda e: e.tensor_copy(cwT_r[:], cwT[:]),
                       waits=[("sV", p.cnt["sV"])], inc="sV")
                cw_done = p.cnt["sV"]
                if dev == "p3r":
                    p.emit("gpsimd", lambda e, hf=hf: e.dma_start(
                        taps["cw"][hf * 128:(hf + 1) * 128, :], cwh[:]),
                        waits=[("sV", cw_done), ("sVs", p.cnt["sVs"])], inc="sO", incn=16)
                    if hf == 1:
                        p.emit("gpsimd", None, waits=[("sO", p.cnt["sO"])])
                        p.flush()
                        return nc
                    continue

                ws_cast = [0, 0]
                gu_mm_done = [0, 0]
                gpsum_free = [0, 0]
                upsum_free = [0, 0]
                sg_free = [0, 0]
                hT_used = 0
                cwb_used = 0
                wd_cast_done = 0
                wd_used = 0
                cwrow_dma = 0
                nexp = NE + 1
                p3flag = ""
                if dev and dev.startswith("p3e"):
                    spec = dev[3:].split(":")
                    nexp = int(spec[0])
                    p3flag = spec[1] if len(spec) > 1 else ""
                for ei in range(nexp):
                    routed = ei > 0
                    eix = ei - 1
                    wgh = wgt_h[eix] if routed else swgt_h
                    wuh = wut_h[eix] if routed else swut_h
                    wdh = wdt_h[eix] if routed else swdt_h
                    if routed and p3flag == "nocw":
                        routed = False
                    if routed:
                        p.emit("gpsimd", lambda e, eix=eix: e.dma_start(
                            cwrow[:], cwT_r[eix:eix + 1, :]),
                            waits=[("sV", max(cw_done, cwb_used))], inc="sX", incn=16)
                        cwrow_dma = p.cnt["sX"]
                        for tt in range(2):
                            p.emit("tensor", lambda e, tt=tt: e.matmul(
                                pb[5][:], ones_r[:], cwrow[:, tt * 512:(tt + 1) * 512],
                                start=True, stop=True),
                                waits=[("sX", cwrow_dma),
                                       ("sV", max(cwb_used, dpsum_free[1], p.cnt["sV"]))],
                                inc="sT")
                            p.emit("vector", lambda e, tt=tt: e.tensor_copy(
                                cw_b[:, tt * 512:(tt + 1) * 512], pb[5][:, 0:512]),
                                waits=[("sT", p.cnt["sT"])], inc="sV")
                        cwb_v = p.cnt["sV"]
                        dpsum_free[1] = max(dpsum_free[1], cwb_v)
                    else:
                        cwb_v = 0
                    for ic in range(8):
                        for wi, whh in ((0, wgh), (1, wuh)):
                            wsb = ws[wi]
                            p.emit("sync", lambda e, wsb=wsb, whh=whh, ic=ic: e.dma_start(
                                wsb[:], whh[ic, :, 0:1024]),
                                waits=[("sV", ws_cast[wi])], inc="sW", incn=16)
                            ld1 = p.cnt["sW"]
                            wgt = wgb[wi]
                            w2 = [("sW", ld1), ("sT", gu_mm_done[wi])]
                            p.emit("vector", lambda e, wgt=wgt, wsb=wsb: e.tensor_copy(
                                wgt[:, 0:8, :].rearrange("p a b -> p (a b)"), wsb[:]),
                                waits=w2, inc="sV")
                            p.emit("sync", lambda e, wsb=wsb, whh=whh, ic=ic: e.dma_start(
                                wsb[:], whh[ic, :, 1024:2048]),
                                waits=[("sV", p.cnt["sV"])], inc="sW", incn=16)
                            ld2 = p.cnt["sW"]
                            p.emit("vector", lambda e, wgt=wgt, wsb=wsb: e.tensor_copy(
                                wgt[:, 8:16, :].rearrange("p a b -> p (a b)"), wsb[:]),
                                waits=[("sW", ld2)], inc="sV")
                            ws_cast[wi] = p.cnt["sV"]
                        gcast = p.cnt["sV"]
                        for tt in range(2):
                            for kc in range(16):
                                w_ = ([("sV", gcast), ("sS", gpsum_free[tt])]
                                      if kc == 0 else [])
                                p.emit("tensor", lambda e, tt=tt, kc=kc: e.matmul(
                                    pb[tt][:], wgb[0][:, kc, :],
                                    xn2T[:, kc, tt * 512:(tt + 1) * 512],
                                    start=(kc == 0), stop=(kc == 15)),
                                    waits=w_, inc="sT" if kc == 15 else None)
                            gmm = p.cnt["sT"]
                            gu_mm_done[0] = gmm
                            for kc in range(16):
                                w_ = [("sV", upsum_free[tt])] if kc == 0 else []
                                p.emit("tensor", lambda e, tt=tt, kc=kc: e.matmul(
                                    pb[2 + tt][:], wgb[1][:, kc, :],
                                    xn2T[:, kc, tt * 512:(tt + 1) * 512],
                                    start=(kc == 0), stop=(kc == 15)),
                                    waits=w_, inc="sT" if kc == 15 else None)
                            umm = p.cnt["sT"]
                            gu_mm_done[1] = umm
                            p.emit("scalar", lambda e, tt=tt: e.activation(
                                sg[tt][:], pb[tt][:], AF.Silu),
                                waits=[("sT", gmm), ("sV", sg_free[tt])], inc="sS")
                            gpsum_free[tt] = p.cnt["sS"]
                            sgv = p.cnt["sS"]
                            hdst = hT[:, ic, tt * 512:(tt + 1) * 512]
                            if routed and p3flag == "nocwh":
                                p.emit("vector", lambda e, hdst=hdst, tt=tt: e.tensor_tensor(
                                    hdst, pb[2 + tt][:], sg[tt][:], op=OP.mult),
                                    waits=[("sT", umm), ("sS", sgv), ("sT", hT_used),
                                           ("sV", cwb_v)], inc="sV")
                                upsum_free[tt] = p.cnt["sV"]
                                sg_free[tt] = p.cnt["sV"]
                                cwb_used = max(cwb_used, p.cnt["sV"])
                                continue
                            if routed:
                                p.emit("vector", lambda e, tt=tt: e.tensor_tensor(
                                    t1b[:], pb[2 + tt][:], cw_b[:, tt * 512:(tt + 1) * 512],
                                    op=OP.mult),
                                    waits=[("sT", umm), ("sV", cwb_v), ("sT", hT_used)],
                                    inc="sV")
                                upsum_free[tt] = p.cnt["sV"]
                                p.emit("vector", lambda e, hdst=hdst, tt=tt: e.tensor_tensor(
                                    hdst, t1b[:], sg[tt][:], op=OP.mult),
                                    waits=[("sS", sgv)], inc="sV")
                            else:
                                p.emit("vector", lambda e, hdst=hdst, tt=tt: e.tensor_tensor(
                                    hdst, pb[2 + tt][:], sg[tt][:], op=OP.mult),
                                    waits=[("sT", umm), ("sS", sgv), ("sT", hT_used)],
                                    inc="sV")
                                upsum_free[tt] = p.cnt["sV"]
                            sg_free[tt] = p.cnt["sV"]
                            cwb_used = max(cwb_used, p.cnt["sV"])
                    h_done = p.cnt["sV"]
                    for ic in range(8):
                        for half in range(2):
                            wsb = ws[half]
                            p.emit("sync", lambda e, wsb=wsb, wdh=wdh, ic=ic, half=half: e.dma_start(
                                wsb[:], wdh[ic, :, half * 1024:(half + 1) * 1024]),
                                waits=[("sV", ws_cast[half])], inc="sW", incn=16)
                            ldd = p.cnt["sW"]
                            p.emit("vector", lambda e, ic=ic, half=half, wsb=wsb: e.tensor_copy(
                                wd3[:, ic, half * 1024:(half + 1) * 1024], wsb[:]),
                                waits=[("sW", ldd), ("sT", wd_used)], inc="sV")
                            ws_cast[half] = p.cnt["sV"]
                    wd_cast_done = p.cnt["sV"]
                    dcnt = 0
                    for c8 in range(8):
                        for ho in range(4):
                            bk = 4 + (dcnt % 2)
                            dcnt += 1
                            for ic in range(8):
                                w_ = ([("sV", max(wd_cast_done, h_done, dpsum_free[bk - 4]))]
                                      if ic == 0 else [])
                                p.emit("tensor", lambda e, bk=bk, ic=ic, c8=c8, ho=ho: e.matmul(
                                    pb[bk][:], hT[:, ic, c8 * 128:(c8 + 1) * 128],
                                    wd3[:, ic, ho * 512:(ho + 1) * 512],
                                    start=(ic == 0), stop=(ic == 7)),
                                    waits=w_, inc="sT" if ic == 7 else None)
                            dmm = p.cnt["sT"]
                            p.emit("vector", lambda e, bk=bk, c8=c8, ho=ho: e.tensor_tensor(
                                oa3[:, c8, ho * 512:(ho + 1) * 512],
                                oa3[:, c8, ho * 512:(ho + 1) * 512],
                                pb[bk][:], op=OP.add),
                                waits=[("sT", dmm)], inc="sV")
                            dpsum_free[bk - 4] = p.cnt["sV"]
                    hT_used = p.cnt["sT"]
                    wd_used = p.cnt["sT"]
                d_all = p.cnt["sV"]
                for c8 in range(8):
                    p.emit("gpsimd", lambda e, c8=c8, base=base: e.dma_start(
                        out_h[base + c8 * 128: base + (c8 + 1) * 128, :], oa3[:, c8, :]),
                        waits=[("sV", d_all)] if c8 == 0 else [], inc="sO", incn=16)
                out_dma_prev = p.cnt["sO"]
                if hf == 0:
                    p.barrier()
            p.emit("gpsimd", None, waits=[("sO", p.cnt["sO"])])
            p.flush()
    return nc



def _host_consts():
    d1 = np.zeros((128, 16, 512), np.float64)
    for di in range(16):
        delta = -1536 + di * 128
        pcol = np.arange(128)[:, None]
        ccol = np.arange(512)[None, :]
        d1[:, di, :] = -np.abs(delta + pcol - ccol) / 8.0
    d1t = d1.reshape(128, 16 * 512).astype(np.float16)

    inv_freq = (1.0 / 10000.0 ** (np.arange(0, DH, 2, dtype=np.float64) / DH))
    t = np.arange(S, dtype=np.float64)[:, None]
    ang = t * inv_freq[None, :]
    cosv = np.cos(ang)
    sinv = np.sin(ang)
    ropec = np.zeros((128, 4, 16, 32), np.float64)
    for tc in range(16):
        rows = slice(tc * 128, (tc + 1) * 128)
        ropec[:, 0, tc, :] = cosv[rows] / 8.0
        ropec[:, 1, tc, :] = sinv[rows] / 8.0
        ropec[:, 2, tc, :] = cosv[rows]
        ropec[:, 3, tc, :] = sinv[rows]
    ropec = ropec.reshape(128, 4 * 16 * 32).astype(np.float32)

    identb = np.eye(128, dtype=ml_dtypes.bfloat16)
    identf = np.eye(128, dtype=np.float32)
    r = np.arange(128)
    triu = (r[None, :] >= r[:, None]).astype(ml_dtypes.bfloat16)
    lowm = np.zeros((NE, NE), np.float32)
    for e in range(NE):
        lowm[e, :e] = 1.0
    lowm = np.broadcast_to(lowm.reshape(1, NE * NE), (128, NE * NE)).copy()
    return d1t, ropec, identb, identf, triu, lowm


def _img_lhsT(w):
    K, M = w.shape
    return np.ascontiguousarray(
        w.reshape(K // 128, 128, M // 128, 128).transpose(2, 1, 0, 3)
    ).reshape(M // 128, 128, K)


def kernel(L0_attn_hidden_states, L0_attn_Wq, L0_attn_Wk, L0_attn_Wv, L0_attn_Wo,
           L0_attn_bo, L0_ffn_gate_w, L0_ffn_expert_bias, L0_ffn_w_gate,
           L0_ffn_w_up, L0_ffn_w_down, L0_ffn_sw_gate, L0_ffn_sw_up,
           L0_ffn_sw_down):
    x = np.ascontiguousarray(np.asarray(L0_attn_hidden_states, dtype=np.float32))
    wq = np.ascontiguousarray(np.asarray(L0_attn_Wq, np.float32))
    wk = np.ascontiguousarray(np.asarray(L0_attn_Wk, np.float32))
    wv = np.ascontiguousarray(np.asarray(L0_attn_Wv, np.float32))
    wo = np.ascontiguousarray(np.asarray(L0_attn_Wo, np.float32))
    bo = np.asarray(L0_attn_bo, np.float32).reshape(1, D)
    gwt = np.ascontiguousarray(np.asarray(L0_ffn_gate_w, np.float32).T)
    eb = np.asarray(L0_ffn_expert_bias, np.float32).reshape(1, NE)
    wgT = np.stack([_img_lhsT(np.asarray(L0_ffn_w_gate[e], np.float32).T)
                    for e in range(NE)])
    wuT = np.stack([_img_lhsT(np.asarray(L0_ffn_w_up[e], np.float32).T)
                    for e in range(NE)])
    wdT = np.stack([np.ascontiguousarray(np.asarray(L0_ffn_w_down[e], np.float32).T)
                    .reshape(8, 128, D) for e in range(NE)])
    swgT = _img_lhsT(np.asarray(L0_ffn_sw_gate, np.float32).T)
    swuT = _img_lhsT(np.asarray(L0_ffn_sw_up, np.float32).T)
    swdT = np.ascontiguousarray(np.asarray(L0_ffn_sw_down, np.float32).T).reshape(8, 128, D)

    d1t, ropec, identb, identf, triu, lowm = _host_consts()

    nc = build()
    shared = {"wq": wq, "wk": wk, "wv": wv, "wo": wo, "bo": bo, "gwt": gwt,
              "ebias": eb, "wgT": wgT, "wuT": wuT, "wdT": wdT,
              "swgT": swgT, "swuT": swuT, "swdT": swdT,
              "d1t": d1t, "ropec": ropec, "identb": identb, "identf": identf,
              "triu": triu, "lowm": lowm}
    in_maps = [dict(shared, x=np.ascontiguousarray(x[b])) for b in range(8)]
    res = run_bass_kernel_spmd(nc, in_maps, core_ids=list(range(8)))
    out = np.stack([np.asarray(res.results[b]["out"], np.float32) for b in range(8)])
    return out


# version 04
# speedup vs baseline: 1.2393x; 1.2393x over previous
import contextlib

import numpy as np
import ml_dtypes

import concourse.bass as bass
import concourse.mybir as mybir
from concourse.bass_utils import run_bass_kernel_spmd
from concourse.alu_op_type import AluOpType

F32 = mybir.dt.float32
F32R = mybir.dt.float32r
F16 = mybir.dt.float16
BF16 = mybir.dt.bfloat16
AF = mybir.ActivationFunctionType
AX = mybir.AxisListType
OP = AluOpType

S = 2048
D = 2048
NT = 16
NH = 32
NG = 8
DH = 64
DI = 1024
NE = 16
TOPK = 4
EPS = 1e-5

ENGINES = ("tensor", "vector", "scalar", "sync", "gpsimd")
SLOPES = [float(np.float32(2.0 ** (-(h + 1) * 8.0 / 32.0))) for h in range(NH)]


class Prog:
    def __init__(self, nc, block):
        self.nc = nc
        self.block = block
        self.ops = {e: [] for e in ENGINES}
        self.cnt = {}
        self.sems = {}

    def mksem(self, stack, name):
        self.sems[name] = stack.enter_context(self.nc.semaphore(name))
        self.cnt[name] = 0

    def emit(self, eng, fn=None, waits=(), inc=None, incn=1):
        w = tuple(waits)
        if inc is not None:
            self.cnt[inc] += incn
        sems = self.sems

        def run(e, fn=fn, w=w, inc=inc, incn=incn, sems=sems):
            for sname, v in w:
                e.wait_ge(sems[sname], v)
            if fn is not None:
                inst = fn(e)
                if inc is not None:
                    inst.then_inc(sems[inc], incn)
            elif inc is not None:
                e.nop().then_inc(sems[inc], incn)

        self.ops[eng].append(run)
        return self.cnt[inc] if inc is not None else None

    def flush(self):
        for e in ENGINES:
            pend = self.ops[e]
            if pend:
                self.ops[e] = []
                getattr(self.block, e)(lambda eng, pend=pend: [f(eng) for f in pend])

    def barrier(self):
        tgt = self.cnt["sB"] + len(ENGINES)
        for e in ENGINES:
            self.emit(e, None, inc="sB")
        for e in ENGINES:
            self.emit(e, None, waits=[("sB", tgt)])


def build(dev=None):
    nc = bass.Bass("TRN2", target_bir_lowering=False)

    x_h = nc.dram_tensor("x", [S, D], F32, kind="ExternalInput").ap()
    wq_h = nc.dram_tensor("wq", [D, D], F32, kind="ExternalInput").ap()
    wk_h = nc.dram_tensor("wk", [D, 512], F32, kind="ExternalInput").ap()
    wv_h = nc.dram_tensor("wv", [D, 512], F32, kind="ExternalInput").ap()
    wo_h = nc.dram_tensor("wo", [D, D], F32, kind="ExternalInput").ap()
    bo_h = nc.dram_tensor("bo", [1, D], F32, kind="ExternalInput").ap()
    gwt_h = nc.dram_tensor("gwt", [D, NE], F32, kind="ExternalInput").ap()
    eb_h = nc.dram_tensor("ebias", [1, NE], F32, kind="ExternalInput").ap()
    wgt_h = nc.dram_tensor("wgT", [NE, 8, 128, 16 * 128], F32, kind="ExternalInput").ap()
    wut_h = nc.dram_tensor("wuT", [NE, 8, 128, 16 * 128], F32, kind="ExternalInput").ap()
    wdt_h = nc.dram_tensor("wdT", [NE, 8, 128, D], F32, kind="ExternalInput").ap()
    swgt_h = nc.dram_tensor("swgT", [8, 128, 16 * 128], F32, kind="ExternalInput").ap()
    swut_h = nc.dram_tensor("swuT", [8, 128, 16 * 128], F32, kind="ExternalInput").ap()
    swdt_h = nc.dram_tensor("swdT", [8, 128, D], F32, kind="ExternalInput").ap()
    d1_h = nc.dram_tensor("d1t", [128, 16 * 512], F16, kind="ExternalInput").ap()
    ropec_h = nc.dram_tensor("ropec", [128, 4 * 16 * 32], F32, kind="ExternalInput").ap()
    identb_h = nc.dram_tensor("identb", [128, 128], BF16, kind="ExternalInput").ap()
    identf_h = nc.dram_tensor("identf", [128, 128], F32, kind="ExternalInput").ap()
    triu_h = nc.dram_tensor("triu", [128, 128], BF16, kind="ExternalInput").ap()
    lowm_h = nc.dram_tensor("lowm", [128, NE * NE], F32, kind="ExternalInput").ap()

    out_h = nc.dram_tensor("out", [S, D], F32, kind="ExternalOutput").ap()
    qr_h = nc.dram_tensor("qr_s", [S, D], BF16,
                          kind="ExternalOutput" if dev == "p0" else "Internal").ap()
    x1_h = nc.dram_tensor("x1_s", [S, D], F32,
                          kind="ExternalOutput" if dev == "p2" else "Internal").ap()
    taps = {}
    if dev == "p0":
        taps["krT"] = nc.dram_tensor("t_krT", [4 * 128, S], BF16, kind="ExternalOutput").ap()
        taps["vext"] = nc.dram_tensor("t_vext", [128, NT * 520], BF16, kind="ExternalOutput").ap()
    if dev and dev.startswith("p1"):
        taps["ctxT"] = nc.dram_tensor("t_ctxT", [128, 16 * S], BF16, kind="ExternalOutput").ap()
    if dev == "p2":
        taps["m2"] = nc.dram_tensor("t_m2", [128, NT], F32, kind="ExternalOutput").ap()
        taps["r2"] = nc.dram_tensor("t_r2", [128, NT], F32, kind="ExternalOutput").ap()
    if dev == "p3r":
        taps["cw"] = nc.dram_tensor("t_cw", [2 * 128, 8 * NE], F32, kind="ExternalOutput").ap()
    if dev and dev.startswith("p3e"):
        pass

    st = contextlib.ExitStack()
    with st:
        en = st.enter_context
        block = en(nc.Block())
        p = Prog(nc, block)
        for sname in ("sW", "sX", "sO", "sT", "sV", "sS", "sVs", "sSs", "sB"):
            p.mksem(st, sname)

        identb = en(nc.sbuf_tensor("c_identb", [128, 128], BF16))
        identf = en(nc.sbuf_tensor("c_identf", [128, 128], F32))
        triu = en(nc.sbuf_tensor("c_triu", [128, 128], BF16))
        onesb = en(nc.sbuf_tensor("c_onesb", [1, 128], BF16))
        onesf = en(nc.sbuf_tensor("c_onesf", [1, 128], F32))
        ones_r = en(nc.sbuf_tensor("c_ones_r", [1, 128], F32R))
        oneshi = en(nc.sbuf_tensor("c_oneshi", [1, 128], F32R))
        bm30 = en(nc.sbuf_tensor("c_bm30", [128, 1], F32))
        bo_bf = en(nc.sbuf_tensor("c_bo_bf", [1, D], BF16))
        gw_f = en(nc.sbuf_tensor("c_gw_f", [128, NE * NE], F32))
        gw_bf = en(nc.sbuf_tensor("c_gw_bf", [128, NE * NE], BF16))
        lowm = en(nc.sbuf_tensor("c_lowm", [128, NE * NE], F32))
        eb_f = en(nc.sbuf_tensor("c_eb_f", [1, NE], F32))
        eb_r = en(nc.sbuf_tensor("c_eb_r", [1, NE], F32R))
        ebb = en(nc.sbuf_tensor("c_ebb", [128, NE], F32))
        m2 = en(nc.sbuf_tensor("c_m2", [128, NT], F32))
        r2 = en(nc.sbuf_tensor("c_r2", [128, NT], F32))
        sums_acc = en(nc.sbuf_tensor("c_sumacc", [128, NT], F32))
        ssq_acc = en(nc.sbuf_tensor("c_ssqacc", [128, NT], F32))
        s1 = en(nc.sbuf_tensor("c_s1", [128, 1], F32))
        mean = en(nc.sbuf_tensor("c_mean", [128, 1], F32))
        ssqs = en(nc.sbuf_tensor("c_ssqs", [128, 1], F32))
        var = en(nc.sbuf_tensor("c_var", [128, 1], F32))
        sdv = en(nc.sbuf_tensor("c_sdv", [128, 1], F32))
        rstd = en(nc.sbuf_tensor("c_rstd", [128, 1], F32))
        sqdump = en(nc.sbuf_tensor("c_sqdump", [128, D], BF16))
        arena = en(nc.sbuf_tensor("c_arena", [128, 16 * S], BF16))
        arena3 = arena[:].rearrange("p (c t) -> p c t", c=16)
        bo_f = arena[0:1, 0:2 * D].bitcast(F32)
        gw3b = gw_bf[:].rearrange("p (k n) -> p k n", k=16)
        lowm3 = lowm[:].rearrange("p (a b) -> p a b", a=NE)

        pb = [en(nc.psum_tensor("pb%d" % i, [128, 512], F32)) for i in range(8)]

        def pbb(i):
            return pb[i][:].bitcast(BF16)

        tslot_ev = [0, 0]

        def emit_transpose(src_ap, dst_ap, ident, extra_wait=None):
            sl = emit_transpose.flip
            emit_transpose.flip ^= 1
            pt = pbb(6 + sl) if src_ap.dtype == BF16 else pb[6 + sl][:]
            np_, nf = src_ap.shape
            w_ = [("sV", tslot_ev[sl])]
            if extra_wait:
                w_.append(extra_wait)
            p.emit("tensor", lambda e, pt=pt, src_ap=src_ap, ident=ident, np_=np_, nf=nf: e.transpose(
                pt[0:nf, 0:np_], src_ap, ident), waits=w_, inc="sT")
            p.emit("vector", lambda e, pt=pt, dst_ap=dst_ap, np_=np_, nf=nf: e.tensor_copy(
                dst_ap, pt[0:nf, 0:np_]), waits=[("sT", p.cnt["sT"])], inc="sV")
            tslot_ev[sl] = p.cnt["sV"]
        emit_transpose.flip = 0

        for dst, src in ((identb[:], identb_h), (identf[:], identf_h), (triu[:], triu_h),
                         (bo_f, bo_h), (lowm[:], lowm_h), (eb_f[:], eb_h)):
            p.emit("sync", lambda e, d=dst, s=src: e.dma_start(d, s), inc="sW", incn=16)
        gwt3 = gwt_h.rearrange("(k p) n -> k p n", p=128)
        gwf3 = gw_f[:].rearrange("p (k n) -> p k n", k=16)
        for kc in range(16):
            p.emit("sync", lambda e, kc=kc: e.dma_start(gwf3[:, kc, :], gwt3[kc]), inc="sW", incn=16)
        wconst = p.cnt["sW"]
        p.emit("vector", lambda e: e.memset(onesb[:], 1.0))
        p.emit("vector", lambda e: e.memset(onesf[:], 1.0))
        p.emit("vector", lambda e: e.memset(bm30[:], -30.0), inc="sVs")
        p.emit("vector", lambda e: e.tensor_copy(ones_r[:], onesf[:]),
               waits=[("sVs", p.cnt["sVs"])], inc="sVs")
        p.emit("vector", lambda e: e.memset(onesf[:, 0:64], 0.0), inc="sVs")
        p.emit("vector", lambda e: e.tensor_copy(oneshi[:], onesf[:]),
               waits=[("sVs", p.cnt["sVs"])], inc="sVs")
        p.emit("vector", lambda e: e.tensor_copy(eb_r[:], eb_f[:]),
               waits=[("sW", wconst), ("sVs", p.cnt["sVs"])], inc="sV")
        p.emit("vector", lambda e: e.tensor_copy(bo_bf[:], bo_f), inc="sV")
        p.emit("vector", lambda e: e.tensor_copy(gw_bf[:], gw_f[:]), inc="sV")
        p.emit("vector", lambda e: e.memset(sums_acc[:], 0.0), inc="sV")
        p.emit("vector", lambda e: e.memset(ssq_acc[:], 0.0), inc="sV")
        vconst = p.cnt["sV"]
        p.emit("tensor", lambda e: e.matmul(pb[6][:, 0:NE], ones_r[:], eb_r[:],
                                            start=True, stop=True),
               waits=[("sV", vconst)], inc="sT")
        p.emit("vector", lambda e: e.tensor_copy(ebb[:], pb[6][:, 0:NE]),
               waits=[("sT", p.cnt["sT"])], inc="sV")
        tslot_ev[0] = p.cnt["sV"]
        p.flush()

        with contextlib.ExitStack() as att:
            ea = att.enter_context
            krt = [ea(nc.sbuf_tensor("krt%d" % i, [128, S], BF16)) for i in range(4)]
            vext = ea(nc.sbuf_tensor("vext", [128, NT * 520], BF16))
            vext3 = vext[:].rearrange("p (t c) -> p t c", t=NT)

            with contextlib.ExitStack() as p0s:
                e0 = p0s.enter_context
                ropec = e0(nc.sbuf_tensor("s_ropec", [128, 4 * 16 * 32], F32))
                ropec4 = ropec[:].rearrange("p (k t i) -> p k t i", k=4, t=16)
                xch = [e0(nc.sbuf_tensor("xch%d" % i, [128, D], F32)) for i in range(2)]
                xc = e0(nc.sbuf_tensor("xc", [128, D], F32))
                xn1c = e0(nc.sbuf_tensor("xn1c", [128, D], BF16))
                ws = [e0(nc.sbuf_tensor("ws%d" % i, [128, 512], F32)) for i in range(2)]
                wub = [e0(nc.sbuf_tensor("wub%d" % i, [128, 16 * 512], BF16)) for i in range(2)]
                rt = [e0(nc.sbuf_tensor("rt%d" % i, [128, 256], F32)) for i in range(8)]
                qrp = [e0(nc.sbuf_tensor("qrp%d" % i, [128, 512], BF16)) for i in range(2)]

                p.emit("sync", lambda e: e.dma_start(ropec[:], ropec_h), inc="sW", incn=16)
                vones = vext3.rearrange("p t (g u) -> p t g u", g=NG)[:, :, :, 64].opt()
                p.emit("vector", lambda e: e.memset(vones, 1.0), inc="sV")

                xc_done = [0] * NT
                prev_var = 0
                for tc in range(NT):
                    bx = xch[tc % 2]
                    w_ = [("sV", xc_done[tc - 2])] if tc >= 2 else []
                    p.emit("gpsimd", lambda e, bx=bx, tc=tc: e.dma_start(
                        bx[:], x_h[tc * 128:(tc + 1) * 128, :]), waits=w_, inc="sX", incn=16)
                    ldx = p.cnt["sX"]
                    p.emit("vector", lambda e, bx=bx: e.reduce_sum(s1[:], bx[:], axis=AX.X),
                           waits=[("sX", ldx), ("sS", p.cnt["sS"])], inc="sVs")
                    p.emit("vector", lambda e: e.tensor_scalar_mul(mean[:], s1[:], 1.0 / D),
                           waits=[("sVs", p.cnt["sVs"])], inc="sVs")
                    p.emit("vector", lambda e, bx=bx: e.tensor_scalar(
                        xc[:], bx[:], mean[:], None, op0=OP.subtract),
                        waits=[("sVs", p.cnt["sVs"])], inc="sV")
                    xc_done[tc] = p.cnt["sV"]
                    p.emit("scalar", lambda e: e.activation(
                        sqdump[:], xc[:], AF.Square, accum_out=ssqs[:]),
                        waits=[("sV", xc_done[tc]), ("sVs", prev_var)], inc="sS")
                    p.emit("vector", lambda e: e.tensor_scalar(
                        var[:], ssqs[:], 1.0 / D, EPS, op0=OP.mult, op1=OP.add),
                        waits=[("sS", p.cnt["sS"])], inc="sVs")
                    prev_var = p.cnt["sVs"]
                    p.emit("scalar", lambda e: e.activation(sdv[:], var[:], AF.Sqrt),
                           waits=[("sVs", p.cnt["sVs"])], inc="sS")
                    p.emit("vector", lambda e: e.reciprocal(rstd[:], sdv[:]),
                           waits=[("sS", p.cnt["sS"])], inc="sVs")
                    p.emit("vector", lambda e: e.tensor_scalar(
                        xn1c[:], xc[:], rstd[:], None, op0=OP.mult),
                        waits=[("sVs", p.cnt["sVs"])], inc="sV")
                    xnv = p.cnt["sV"]
                    for fc in range(16):
                        emit_transpose(xn1c[:, fc * 128:(fc + 1) * 128],
                                       arena3[:, fc, tc * 128:(tc + 1) * 128],
                                       identb[:], extra_wait=("sV", xnv))
                xn1_all = p.cnt["sV"]

                units = [("k", wk_h, 0), ("v", wv_h, 0), ("q", wq_h, 0),
                         ("q", wq_h, 1), ("q", wq_h, 2), ("q", wq_h, 3)]
                unit_mm_done = [0] * len(units)
                cast_done = [0] * len(units)
                ws_cast = [0, 0]
                bank_free = [0, 0]
                qrp_free = [("sV", 0), ("sV", 0)]
                for ui, (kind, wh, fo) in enumerate(units):
                    ub = wub[ui % 2]
                    ub3 = ub[:].rearrange("p (k n) -> p k n", k=16)
                    for kc in range(16):
                        wsb = ws[kc % 2]
                        p.emit("sync", lambda e, wsb=wsb, wh=wh, kc=kc, fo=fo: e.dma_start(
                            wsb[:], wh[kc * 128:(kc + 1) * 128, fo * 512:(fo + 1) * 512]),
                            waits=[("sV", ws_cast[kc % 2])], inc="sW", incn=16)
                        ldw = p.cnt["sW"]
                        w2 = [("sW", ldw)]
                        if ui >= 2 and kc == 0:
                            w2.append(("sT", unit_mm_done[ui - 2]))
                        p.emit("vector", lambda e, ub3=ub3, kc=kc, wsb=wsb: e.tensor_copy(
                            ub3[:, kc, :], wsb[:]), waits=w2, inc="sV")
                        ws_cast[kc % 2] = p.cnt["sV"]
                    cast_done[ui] = p.cnt["sV"]
                    for tc in range(NT):
                        bk = (ui * NT + tc) % 2
                        ps = pb[bk]
                        for kc in range(16):
                            w_ = ([("sV", max(cast_done[ui], xn1_all, bank_free[bk]))]
                                  if kc == 0 else [])
                            p.emit("tensor", lambda e, ps=ps, kc=kc, tc=tc, ub3=ub3: e.matmul(
                                ps[:], arena3[:, kc, tc * 128:(tc + 1) * 128], ub3[:, kc, :],
                                start=(kc == 0), stop=(kc == 15)),
                                waits=w_, inc="sT" if kc == 15 else None)
                        mmd = p.cnt["sT"]
                        if kind == "v":
                            vdst = vext3[:, tc, :].rearrange("p (g u) -> p g u", g=NG)[:, :, 0:64]
                            psrc = ps[:].rearrange("p (g u) -> p g u", g=NG)
                            p.emit("vector", lambda e, vdst=vdst, psrc=psrc: e.tensor_copy(
                                vdst, psrc), waits=[("sT", mmd)], inc="sV")
                            bank_free[bk] = p.cnt["sV"]
                        else:
                            ti = 0 if kind == "q" else 2
                            cq = ropec4[:, ti, tc, :].unsqueeze(1).broadcast_to([128, 8, 32])
                            sq = ropec4[:, ti + 1, tc, :].unsqueeze(1).broadcast_to([128, 8, 32])
                            p4 = ps[:].rearrange("p (h i two) -> p h i two", h=8, two=2)
                            pev = p4[:, :, :, 0].opt()
                            pod = p4[:, :, :, 1].opt()
                            rb = (ui * NT + tc) % 2
                            t1, t2, t3, t4 = (rt[4 * rb + i][:].rearrange(
                                "p (h i) -> p h i", h=8) for i in range(4))
                            qb = qrp[rb]
                            q4 = qb[:].rearrange("p (h i two) -> p h i two", h=8, two=2)
                            qev = q4[:, :, :, 0].opt()
                            qod = q4[:, :, :, 1].opt()
                            wqr = [("sT", mmd), qrp_free[rb]]
                            p.emit("vector", lambda e, t1=t1, pev=pev, cq=cq: e.tensor_tensor(
                                t1, pev, cq, op=OP.mult), waits=wqr, inc="sV")
                            p.emit("vector", lambda e, t2=t2, pod=pod, sq=sq: e.tensor_tensor(
                                t2, pod, sq, op=OP.mult), inc="sV")
                            p.emit("vector", lambda e, t3=t3, pev=pev, sq=sq: e.tensor_tensor(
                                t3, pev, sq, op=OP.mult), inc="sV")
                            p.emit("vector", lambda e, t4=t4, pod=pod, cq=cq: e.tensor_tensor(
                                t4, pod, cq, op=OP.mult), inc="sV")
                            bank_free[bk] = p.cnt["sV"]
                            p.emit("vector", lambda e, qev=qev, t1=t1, t2=t2: e.tensor_tensor(
                                qev, t1, t2, op=OP.subtract), inc="sV")
                            p.emit("vector", lambda e, qod=qod, t3=t3, t4=t4: e.tensor_tensor(
                                qod, t3, t4, op=OP.add), inc="sV")
                            rpv = p.cnt["sV"]
                            if kind == "q":
                                p.emit("gpsimd", lambda e, qb=qb, tc=tc, fo=fo: e.dma_start(
                                    qr_h[tc * 128:(tc + 1) * 128, fo * 512:(fo + 1) * 512],
                                    qb[:]), waits=[("sV", rpv)], inc="sO", incn=16)
                                qrp_free[rb] = ("sO", p.cnt["sO"])
                            else:
                                for fb in range(4):
                                    emit_transpose(qb[:, fb * 128:(fb + 1) * 128],
                                                   krt[fb][:, tc * 128:(tc + 1) * 128],
                                                   identb[:], extra_wait=("sV", rpv))
                                qrp_free[rb] = ("sV", p.cnt["sV"])
                        unit_mm_done[ui] = p.cnt["sT"]
                qr_spill_done = p.cnt["sO"]

                if dev == "p0":
                    wfin = [("sV", p.cnt["sV"]), ("sT", p.cnt["sT"]), ("sO", qr_spill_done)]
                    for i in range(4):
                        p.emit("gpsimd", lambda e, i=i: e.dma_start(
                            taps["krT"][i * 128:(i + 1) * 128, :], krt[i][:]),
                            waits=wfin if i == 0 else [], inc="sO", incn=16)
                    p.emit("gpsimd", lambda e: e.dma_start(taps["vext"], vext[:]),
                           inc="sO", incn=16)
                    p.emit("gpsimd", None, waits=[("sO", p.cnt["sO"])])
                    p.flush()
                    return nc
                p.barrier()
                p.flush()

            with contextlib.ExitStack() as p1s:
                e1 = p1s.enter_context
                d1t = e1(nc.sbuf_tensor("s_d1t", [128, 16 * 512], F16))
                d1t3 = d1t[:].rearrange("p (a b) -> p a b", a=16)
                ub_ = [e1(nc.sbuf_tensor("ub%d" % i, [128, S], F32)) for i in range(2)]
                pbf = [e1(nc.sbuf_tensor("pbf%d" % i, [128, S], BF16)) for i in range(2)]
                recT = e1(nc.sbuf_tensor("recT", [128, 512], F32))
                recR = e1(nc.sbuf_tensor("recR", [32, S], F32R))
                recrow_t = e1(nc.sbuf_tensor("recrow", [1, S], F32R))
                sumstg = e1(nc.sbuf_tensor("sumstg", [128, S], F32))
                sums = sumstg
                ostg = [e1(nc.sbuf_tensor("ostg%d" % i, [128, 512], BF16)) for i in range(2)]
                recrow = recrow_t[:]

                qhead = [e1(nc.sbuf_tensor("qhead%d" % i, [64, S], BF16)) for i in range(2)]
                khead = [e1(nc.sbuf_tensor("khead%d" % i, [64, S], BF16)) for i in range(2)]
                p.emit("sync", lambda e: e.dma_start(d1t[:], d1_h), inc="sW", incn=16)
                d1_ld = p.cnt["sW"]

                with contextlib.ExitStack() as pqs:
                    qrc = [pqs.enter_context(nc.sbuf_tensor("qrc%d" % i, [128, S], BF16))
                           for i in range(2)]
                    qrc_free = [0, 0]
                    for tc in range(NT):
                        qb = qrc[tc % 2]
                        w_ = [("sV", qrc_free[tc % 2])]
                        if tc == 0:
                            w_.append(("sO", qr_spill_done))
                        p.emit("gpsimd", lambda e, qb=qb, tc=tc: e.dma_start(
                            qb[:], qr_h[tc * 128:(tc + 1) * 128, :]), waits=w_,
                            inc="sX", incn=16)
                        ldq = p.cnt["sX"]
                        for fc in range(16):
                            emit_transpose(qb[:, fc * 128:(fc + 1) * 128],
                                           arena3[:, fc, tc * 128:(tc + 1) * 128],
                                           identb[:], extra_wait=("sX", ldq))
                        qrc_free[tc % 2] = p.cnt["sV"]
                    qrT_all = p.cnt["sV"]
                    p.flush()
                if dev == "p1qr":
                    p.emit("gpsimd", lambda e: e.dma_start(taps["ctxT"], arena[:]),
                           waits=[("sV", p.cnt["sV"])], inc="sO", incn=16)
                    p.emit("gpsimd", None, waits=[("sO", p.cnt["sO"])])
                    p.flush()
                    return nc

                sbank_free = [0, 0]
                ub_free = [0, 0]
                pbuf_free = [0, 0]
                cb_ev = [0] * 4
                ctx_stop = [0] * 4
                sct = 0
                qh_used = [0, 0]
                kh_used = [0, 0]
                stg_used = [("sV", 0), ("sV", 0)]
                sumstg_used = 0
                sums_dma = 0
                nheads = NH
                if dev and dev.startswith("p1h"):
                    nheads = int(dev[3:])
                for h in range(nheads):
                    g = h // 4
                    slope = SLOPES[h]
                    fcq, rq = h // 2, (h % 2) * 64
                    hb = h % 2
                    p.emit("gpsimd", lambda e, hb=hb, rq=rq, fcq=fcq: e.dma_start(
                        qhead[hb][:], arena3[rq:rq + 64, fcq, :]),
                        waits=[("sV", qrT_all), ("sT", qh_used[hb])], inc="sX", incn=16)
                    qh_ld = p.cnt["sX"]
                    kb = (h // 4) % 2
                    if h % 4 == 0:
                        rk = (g % 2) * 64
                        p.emit("gpsimd", lambda e, kb=kb, g=g, rk=rk: e.dma_start(
                            khead[kb][:], krt[g // 2][rk:rk + 64, :]),
                            waits=[("sT", kh_used[kb])], inc="sX", incn=16)
                    kh_ld = p.cnt["sX"]
                    def emit_ctx(kc, maskv):
                        j0 = kc // 4
                        qs0 = kc * 128
                        par = kc % 2
                        for j in range(j0, 4):
                            c0 = max(qs0, j * 512)
                            w = (j + 1) * 512 - c0
                            kl = min(15, 4 * j + 3)
                            wts = [("sV", maskv)]
                            if kc == 0:
                                wts.append(("sV", cb_ev[j]))
                                wts.append(("sO", sums_dma))
                            p.emit("tensor", lambda e, j=j, c0=c0, w=w, kc=kc, par=par, g=g: e.matmul(
                                pb[2 + j][0:65, c0 - j * 512:c0 - j * 512 + w],
                                vext3[:, kc, g * 65:g * 65 + 65],
                                pbf[par][:, c0:c0 + w],
                                start=(kc == 0), stop=(kc == kl)), waits=wts, inc="sT")
                            if kc == kl:
                                ctx_stop[j] = p.cnt["sT"]
                        pbuf_free[par] = p.cnt["sT"]

                    pend = None
                    for kc in range(16):
                        j0 = kc // 4
                        qs0 = kc * 128
                        par = kc % 2
                        for j in range(j0, 4):
                            c0 = max(qs0, j * 512)
                            w = (j + 1) * 512 - c0
                            bk = sct % 2
                            sct += 1
                            wts = [("sV", sbank_free[bk]), ("sX", max(qh_ld, kh_ld))]
                            p.emit("tensor", lambda e, bk=bk, w=w, kc=kc, c0=c0, hb=hb, kb=kb: e.matmul(
                                pb[bk][:, 0:w],
                                khead[kb][:, kc * 128:(kc + 1) * 128],
                                qhead[hb][:, c0:c0 + w],
                                start=True, stop=True), waits=wts, inc="sT")
                            mmv = p.cnt["sT"]
                            di = (kc * 128 - j * 512 + 1536) // 128
                            off = c0 - j * 512
                            wts = [("sT", mmv), ("sS", ub_free[par])]
                            if kc <= 1 and h == 0:
                                wts.append(("sW", d1_ld))
                            p.emit("vector", lambda e, par=par, c0=c0, w=w, di=di, off=off,
                                   slope=slope, bk=bk: e.scalar_tensor_tensor(
                                ub_[par][:, c0:c0 + w], d1t3[:, di, off:off + w], slope,
                                pb[bk][:, 0:w], op0=OP.mult, op1=OP.add),
                                waits=wts, inc="sV")
                            sbank_free[bk] = p.cnt["sV"]
                        qh_used[hb] = p.cnt["sT"]
                        kh_used[kb] = p.cnt["sT"]
                        sttv = p.cnt["sV"]
                        p.emit("scalar", lambda e, par=par, qs0=qs0: e.activation(
                            ub_[par][:, qs0:S], ub_[par][:, qs0:S], AF.Tanh, scale=1.0 / 30.0),
                            waits=[("sV", sttv), ("sT", pbuf_free[par])])
                        p.emit("scalar", lambda e, par=par, qs0=qs0: e.activation(
                            pbf[par][:, qs0:S], ub_[par][:, qs0:S], AF.Exp,
                            bias=bm30[:], scale=30.0), inc="sS")
                        ub_free[par] = p.cnt["sS"]
                        expv = p.cnt["sS"]
                        p.emit("vector", lambda e, par=par, qs0=qs0: e.tensor_tensor(
                            pbf[par][:, qs0:qs0 + 128], pbf[par][:, qs0:qs0 + 128],
                            triu[:], op=OP.mult), waits=[("sS", expv)], inc="sV")
                        if pend is not None:
                            emit_ctx(*pend)
                        pend = (kc, p.cnt["sV"])
                    emit_ctx(*pend)
                    for j in range(4):
                        if rq == 0:
                            p.emit("vector", lambda e, j=j, fcq=fcq: e.tensor_copy(
                                arena3[0:64, fcq, j * 512:(j + 1) * 512], pb[2 + j][0:64, :]),
                                waits=[("sT", ctx_stop[j])], inc="sV")
                        else:
                            sg_ = j % 2
                            p.emit("vector", lambda e, j=j, sg_=sg_: e.tensor_copy(
                                ostg[sg_][:] if False else ostg[sg_][0:64, :], pb[2 + j][0:64, :]),
                                waits=[("sT", ctx_stop[j]), stg_used[sg_]], inc="sV")
                            p.emit("gpsimd", lambda e, j=j, fcq=fcq, sg_=sg_: e.dma_start(
                                arena3[64:128, fcq, j * 512:(j + 1) * 512], ostg[sg_][0:64, :]),
                                waits=[("sV", p.cnt["sV"])], inc="sX", incn=16)
                            stg_used[sg_] = ("sX", p.cnt["sX"])
                        p.emit("vector", lambda e, j=j: e.tensor_copy(
                            sumstg[64:65, j * 512:(j + 1) * 512], pb[2 + j][64:65, :]),
                            waits=([("sT", ctx_stop[j])] if rq == 0 else []) +
                                  ([("sO", sums_dma)] if j == 0 else []), inc="sV")
                        cb_ev[j] = p.cnt["sV"]
                    p.emit("gpsimd", lambda e, h=h: e.dma_start(
                        sums[h:h + 1, :].bitcast(F32) if False else sums[h:h + 1, :],
                        sumstg[64:65, :]), waits=[("sV", cb_ev[3])], inc="sO", incn=16)
                    sums_dma = p.cnt["sO"]
                ctx_all_ev = max(p.cnt["sV"], 0)
                stg_dma_all = p.cnt["sX"]
                if dev and dev.startswith("p1h"):
                    p.emit("gpsimd", lambda e: e.dma_start(taps["ctxT"], arena[:]),
                           waits=[("sV", p.cnt["sV"]), ("sX", stg_dma_all)], inc="sO", incn=16)
                    p.emit("gpsimd", None, waits=[("sO", p.cnt["sO"])])
                    p.flush()
                    return nc

                for tc in range(NT):
                    sl = emit_transpose.flip
                    emit_transpose.flip ^= 1
                    pt = pb[6 + sl]
                    p.emit("tensor", lambda e, pt=pt, tc=tc: e.transpose(
                        pt[0:128, 0:32], sums[0:32, tc * 128:(tc + 1) * 128],
                        identf[0:32, 0:32]),
                        waits=[("sV", tslot_ev[sl]), ("sO", sums_dma)], inc="sT")
                    p.emit("vector", lambda e, pt=pt, tc=tc: e.reciprocal(
                        recT[:, tc * 32:(tc + 1) * 32], pt[0:128, 0:32]),
                        waits=[("sT", p.cnt["sT"])], inc="sV")
                    tslot_ev[sl] = p.cnt["sV"]
                recT_done = p.cnt["sV"]
                for tc in range(NT):
                    sl = emit_transpose.flip
                    emit_transpose.flip ^= 1
                    pt = pb[6 + sl]
                    p.emit("tensor", lambda e, pt=pt, tc=tc: e.transpose(
                        pt[0:32, 0:128], recT[:, tc * 32:(tc + 1) * 32], identf[:]),
                        waits=[("sV", max(tslot_ev[sl], recT_done))], inc="sT")
                    p.emit("vector", lambda e, pt=pt, tc=tc: e.tensor_copy(
                        recR[0:32, tc * 128:(tc + 1) * 128], pt[0:32, 0:128]),
                        waits=[("sT", p.cnt["sT"])], inc="sV")
                    tslot_ev[sl] = p.cnt["sV"]
                recR_done = p.cnt["sV"]
                rr_dma = 0
                for h in range(NH):
                    fcq, rq = h // 2, (h % 2) * 64
                    p.emit("gpsimd", lambda e, h=h: e.dma_start(
                        recrow, recR[h:h + 1, :]),
                        waits=[("sV", max(recR_done, p.cnt["sV"])), ("sX", stg_dma_all)],
                        inc="sX", incn=16)
                    rr_dma = p.cnt["sX"]
                    for j in range(4):
                        sl = emit_transpose.flip
                        emit_transpose.flip ^= 1
                        pt = pb[6 + sl]
                        lt = ones_r[:, 0:64] if rq == 0 else oneshi[:]
                        mrows = 64 if rq == 0 else 128
                        p.emit("tensor", lambda e, pt=pt, lt=lt, mrows=mrows, j=j: e.matmul(
                            pt[0:mrows, :], lt, recrow[0:1, j * 512:(j + 1) * 512],
                            start=True, stop=True),
                            waits=[("sV", tslot_ev[sl]), ("sX", rr_dma)], inc="sT")
                        p.emit("vector", lambda e, pt=pt, rq=rq, fcq=fcq, j=j: e.tensor_tensor(
                            arena3[rq:rq + 64, fcq, j * 512:(j + 1) * 512],
                            arena3[rq:rq + 64, fcq, j * 512:(j + 1) * 512],
                            pt[rq:rq + 64, :], op=OP.mult),
                            waits=[("sT", p.cnt["sT"])], inc="sV")
                        tslot_ev[sl] = p.cnt["sV"]

                if dev == "p1":
                    p.emit("gpsimd", lambda e: e.dma_start(taps["ctxT"], arena[:]),
                           waits=[("sV", p.cnt["sV"])], inc="sO", incn=16)
                    p.emit("gpsimd", None, waits=[("sO", p.cnt["sO"])])
                    p.flush()
                    return nc
                p.barrier()
                p.flush()

        with contextlib.ExitStack() as p2s:
            e2 = p2s.enter_context
            ws = [e2(nc.sbuf_tensor("w2s%d" % i, [128, 512], F32)) for i in range(2)]
            wub = [e2(nc.sbuf_tensor("w2ub%d" % i, [128, 16 * 512], BF16)) for i in range(2)]
            xsl = [e2(nc.sbuf_tensor("xsl%d" % i, [128, 512], F32)) for i in range(2)]
            x1p = [e2(nc.sbuf_tensor("x1p%d" % i, [128, 512], F32)) for i in range(2)]
            ws_cast = [0, 0]
            unit_mm_done = [0] * 4
            bank_ev = [0, 0]
            x1_free = [0, 0]
            xsl_used = [0, 0]
            for fo in range(4):
                ub = wub[fo % 2]
                ub3 = ub[:].rearrange("p (k n) -> p k n", k=16)
                for kc in range(16):
                    wsb = ws[kc % 2]
                    p.emit("sync", lambda e, wsb=wsb, kc=kc, fo=fo: e.dma_start(
                        wsb[:], wo_h[kc * 128:(kc + 1) * 128, fo * 512:(fo + 1) * 512]),
                        waits=[("sV", ws_cast[kc % 2])], inc="sW", incn=16)
                    ldw = p.cnt["sW"]
                    w2 = [("sW", ldw)]
                    if fo >= 2 and kc == 0:
                        w2.append(("sT", unit_mm_done[fo - 2]))
                    p.emit("vector", lambda e, ub3=ub3, kc=kc, wsb=wsb: e.tensor_copy(
                        ub3[:, kc, :], wsb[:]), waits=w2, inc="sV")
                    ws_cast[kc % 2] = p.cnt["sV"]
                castd = p.cnt["sV"]
                for tc in range(NT):
                    bk = (fo * NT + tc) % 2
                    ps = pb[bk]
                    p.emit("gpsimd", lambda e, bk=bk, tc=tc, fo=fo: e.dma_start(
                        xsl[bk][:], x_h[tc * 128:(tc + 1) * 128, fo * 512:(fo + 1) * 512]),
                        waits=[("sV", xsl_used[bk])], inc="sX", incn=16)
                    ldx = p.cnt["sX"]
                    for kc in range(16):
                        w_ = [("sV", max(castd, bank_ev[bk]))] if kc == 0 else []
                        p.emit("tensor", lambda e, ps=ps, kc=kc, tc=tc, ub3=ub3: e.matmul(
                            ps[:], arena3[:, kc, tc * 128:(tc + 1) * 128], ub3[:, kc, :],
                            start=(kc == 0), stop=False), waits=w_)
                    p.emit("tensor", lambda e, ps=ps, fo=fo: e.matmul(
                        ps[:], onesb[:], bo_bf[:, fo * 512:(fo + 1) * 512],
                        start=False, stop=True), inc="sT")
                    mmd = p.cnt["sT"]
                    p.emit("vector", lambda e, ps=ps, bk=bk: e.tensor_tensor(
                        x1p[bk][:], ps[:], xsl[bk][:], op=OP.add),
                        waits=[("sT", mmd), ("sX", ldx), ("sO", x1_free[bk]),
                               ("sS", p.cnt["sS"])], inc="sV")
                    bank_ev[bk] = p.cnt["sV"]
                    xsl_used[bk] = p.cnt["sV"]
                    x1v = p.cnt["sV"]
                    p.emit("vector", lambda e, bk=bk: e.reduce_sum(
                        s1[:], x1p[bk][:], axis=AX.X), inc="sVs")
                    p.emit("vector", lambda e, tc=tc: e.tensor_tensor(
                        sums_acc[:, tc:tc + 1], sums_acc[:, tc:tc + 1], s1[:], op=OP.add),
                        waits=[("sVs", p.cnt["sVs"])], inc="sVs")
                    p.emit("scalar", lambda e, bk=bk: e.activation(
                        sqdump[:, 0:512], x1p[bk][:], AF.Square, accum_out=ssqs[:]),
                        waits=[("sV", x1v), ("sVs", p.cnt["sVs"])], inc="sS")
                    p.emit("vector", lambda e, tc=tc: e.tensor_tensor(
                        ssq_acc[:, tc:tc + 1], ssq_acc[:, tc:tc + 1], ssqs[:], op=OP.add),
                        waits=[("sS", p.cnt["sS"]), ("sVs", p.cnt["sVs"])], inc="sVs")
                    p.emit("gpsimd", lambda e, bk=bk, tc=tc, fo=fo: e.dma_start(
                        x1_h[tc * 128:(tc + 1) * 128, fo * 512:(fo + 1) * 512], x1p[bk][:]),
                        waits=[("sV", x1v)], inc="sO", incn=16)
                    x1_free[bk] = p.cnt["sO"]
                unit_mm_done[fo] = p.cnt["sT"]
            x1_all = p.cnt["sO"]
            sall = p.cnt["sVs"]
            p.emit("vector", lambda e: e.tensor_scalar_mul(m2[:], sums_acc[:], 1.0 / D),
                   waits=[("sVs", sall)], inc="sVs")
            p.emit("vector", lambda e: e.tensor_scalar_mul(r2[:], ssq_acc[:], 1.0 / D),
                   waits=[("sVs", p.cnt["sVs"])], inc="sVs")
            p.emit("vector", lambda e: e.tensor_tensor(
                sums_acc[:], m2[:], m2[:], op=OP.mult),
                waits=[("sVs", p.cnt["sVs"])], inc="sVs")
            p.emit("vector", lambda e: e.tensor_tensor(
                r2[:], r2[:], sums_acc[:], op=OP.subtract),
                waits=[("sVs", p.cnt["sVs"])], inc="sVs")
            p.emit("vector", lambda e: e.tensor_scalar_add(r2[:], r2[:], EPS),
                   waits=[("sVs", p.cnt["sVs"])], inc="sVs")
            p.emit("scalar", lambda e: e.activation(ssq_acc[:], r2[:], AF.Sqrt),
                   waits=[("sVs", p.cnt["sVs"])], inc="sS")
            p.emit("vector", lambda e: e.reciprocal(r2[:], ssq_acc[:]),
                   waits=[("sS", p.cnt["sS"])], inc="sVs")
            p.emit("vector", None, waits=[("sVs", p.cnt["sVs"])], inc="sV")
            r2_done = p.cnt["sV"]
            if dev == "p2":
                p.emit("gpsimd", lambda e: e.dma_start(taps["m2"], m2[:]),
                       waits=[("sV", r2_done), ("sO", x1_all)], inc="sO", incn=16)
                p.emit("gpsimd", lambda e: e.dma_start(taps["r2"], r2[:]), inc="sO", incn=16)
                p.emit("gpsimd", None, waits=[("sO", p.cnt["sO"])])
                p.flush()
                return nc
            p.barrier()
            p.flush()

        with contextlib.ExitStack() as p3s:
            e3 = p3s.enter_context
            out_acc = e3(nc.sbuf_tensor("out_acc", [128, 8 * D], F32))
            oa3 = out_acc[:].rearrange("p (c t) -> p c t", c=8)
            p3wd = e3(nc.sbuf_tensor("p3wd", [128, 8 * D], BF16))
            wd3 = p3wd[:].rearrange("p (c t) -> p c t", c=8)
            ws = [e3(nc.sbuf_tensor("w3s%d" % i, [128, 1024], F32)) for i in range(2)]
            sg = [e3(nc.sbuf_tensor("sg%d" % i, [128, 512], BF16)) for i in range(2)]
            t1b = e3(nc.sbuf_tensor("t1b", [128, 512], F32))
            cw_b = e3(nc.sbuf_tensor("cw_b", [128, 1024], F32))
            cwT = e3(nc.sbuf_tensor("cwT", [16, 1024], F32))
            cwT_r = e3(nc.sbuf_tensor("cwT_r", [16, 1024], F32R))
            cwrow = e3(nc.sbuf_tensor("cwrow", [1, 1024], F32R))
            cwh = e3(nc.sbuf_tensor("cwh", [128, 8 * NE], F32))
            rscr = e3(nc.sbuf_tensor("rscr", [128, 1024], F32))
            gtall = rscr[:, 0:256].rearrange("p (a b) -> p a b", a=NE)
            eqall = rscr[:, 256:512].rearrange("p (a b) -> p a b", a=NE)
            beats = rscr[:, 512:768].rearrange("p (a b) -> p a b", a=NE)
            scoresb = rscr[:, 768:784]
            selb = rscr[:, 784:800]
            rankb = rscr[:, 800:816]
            maskb = rscr[:, 816:832]
            wraw = rscr[:, 832:848]
            rsum = rscr[:, 848:849]
            rinv = rscr[:, 849:850]
            xn2T = arena[:, 0:16 * 1024].rearrange("p (c t) -> p c t", c=16)
            hT = arena[:, 16 * 1024:24 * 1024].rearrange("p (c t) -> p c t", c=8)
            wgb = [arena[:, 24 * 1024 + i * 2048: 24 * 1024 + (i + 1) * 2048]
                   .rearrange("p (k n) -> p k n", k=16) for i in range(4)]

            out_dma_prev = 0
            for hf in range(2):
                base = hf * 1024
                ldh = []
                for c in range(8):
                    p.emit("gpsimd", lambda e, c=c, base=base: e.dma_start(
                        oa3[:, c, :], x1_h[base + c * 128: base + (c + 1) * 128, :]),
                        waits=([("sO", x1_all), ("sO", out_dma_prev)] if c == 0 else []),
                        inc="sX", incn=16)
                    ldh.append(p.cnt["sX"])
                xn2_done = 0
                for c in range(8):
                    gcol = hf * 8 + c
                    w_ = [("sX", ldh[c]), ("sT", xn2_done), ("sV", r2_done)]
                    p.emit("vector", lambda e, c=c, gcol=gcol: e.tensor_scalar(
                        sqdump[:], oa3[:, c, :], m2[:, gcol:gcol + 1], r2[:, gcol:gcol + 1],
                        op0=OP.subtract, op1=OP.mult), waits=w_, inc="sV")
                    xnv = p.cnt["sV"]
                    for fc in range(16):
                        emit_transpose(sqdump[:, fc * 128:(fc + 1) * 128],
                                       xn2T[:, fc, c * 128:(c + 1) * 128],
                                       identb[:], extra_wait=("sV", xnv))
                    xn2_done = p.cnt["sT"]
                    xn2v = p.cnt["sV"]
                    for kc in range(16):
                        w_ = [("sV", xn2v), ("sS", p.cnt["sS"])] if kc == 0 else []
                        p.emit("tensor", lambda e, kc=kc, c=c: e.matmul(
                            pb[5][:, 0:NE], xn2T[:, kc, c * 128:(c + 1) * 128],
                            gw3b[:, kc, :], start=(kc == 0), stop=(kc == 15)),
                            waits=w_, inc="sT" if kc == 15 else None)
                    lgd = p.cnt["sT"]
                    p.emit("scalar", lambda e: e.activation(scoresb, pb[5][:, 0:NE], AF.Sigmoid),
                           waits=[("sT", lgd), ("sVs", p.cnt["sVs"])], inc="sS")
                    p.emit("vector", lambda e: e.tensor_tensor(selb, scoresb, ebb[:], op=OP.add),
                           waits=[("sS", p.cnt["sS"])], inc="sVs")
                    selv = p.cnt["sVs"]
                    for ee in range(NE):
                        p.emit("vector", lambda e, ee=ee: e.tensor_scalar(
                            gtall[:, ee, :], selb, selb[:, ee:ee + 1], None, op0=OP.is_gt),
                            waits=[("sVs", selv)] if ee == 0 else [])
                        p.emit("vector", lambda e, ee=ee: e.tensor_scalar(
                            eqall[:, ee, :], selb, selb[:, ee:ee + 1], None, op0=OP.is_equal),
                            inc="sVs" if ee == NE - 1 else None)
                    p.emit("vector", lambda e: e.tensor_tensor(eqall, eqall, lowm3, op=OP.mult),
                           waits=[("sVs", p.cnt["sVs"])], inc="sVs")
                    p.emit("vector", lambda e: e.tensor_tensor(beats, gtall, eqall, op=OP.add),
                           waits=[("sVs", p.cnt["sVs"])], inc="sVs")
                    p.emit("vector", lambda e: e.reduce_sum(
                        rankb.rearrange("p (a o) -> p a o", o=1), beats, axis=AX.X),
                        waits=[("sVs", p.cnt["sVs"])], inc="sVs")
                    p.emit("vector", lambda e: e.tensor_single_scalar(
                        maskb, rankb, float(TOPK), op=OP.is_lt),
                        waits=[("sVs", p.cnt["sVs"])], inc="sVs")
                    p.emit("vector", lambda e: e.tensor_tensor(wraw, maskb, scoresb, op=OP.mult),
                           waits=[("sVs", p.cnt["sVs"])], inc="sVs")
                    p.emit("vector", lambda e: e.reduce_sum(rsum, wraw, axis=AX.X),
                           waits=[("sVs", p.cnt["sVs"])], inc="sVs")
                    p.emit("vector", lambda e: e.reciprocal(rinv, rsum),
                           waits=[("sVs", p.cnt["sVs"])], inc="sVs")
                    p.emit("vector", lambda e, c=c: e.tensor_scalar(
                        cwh[:, c * NE:(c + 1) * NE], wraw, rinv, None, op0=OP.mult),
                        waits=[("sVs", p.cnt["sVs"])], inc="sVs")
                    cwv = p.cnt["sVs"]
                    emit_transpose(cwh[:, c * NE:(c + 1) * NE],
                                   cwT[:, c * 128:(c + 1) * 128], identf[:],
                                   extra_wait=("sVs", cwv))
                p.emit("vector", lambda e: e.tensor_copy(cwT_r[:], cwT[:]),
                       waits=[("sV", p.cnt["sV"])], inc="sV")
                cw_done = p.cnt["sV"]
                if dev == "p3r":
                    p.emit("gpsimd", lambda e, hf=hf: e.dma_start(
                        taps["cw"][hf * 128:(hf + 1) * 128, :], cwh[:]),
                        waits=[("sV", cw_done), ("sVs", p.cnt["sVs"])], inc="sO", incn=16)
                    if hf == 1:
                        p.emit("gpsimd", None, waits=[("sO", p.cnt["sO"])])
                        p.flush()
                        return nc
                    continue

                ws_cast = [0, 0]
                slot_mm_done = [0, 0, 0, 0]
                slot_cast = [0, 0, 0, 0]
                gpsum_free = [0, 0]
                upsum_free = [0, 0]
                dband_free = [0, 0, 0, 0]
                sg_free = [0, 0]
                hT_used = 0
                cwb_used = 0
                wd_cast_done = 0
                wd_used = 0
                cwrow_dma = 0
                nexp = NE + 1
                p3flag = ""
                if dev and dev.startswith("p3e"):
                    spec = dev[3:].split(":")
                    nexp = int(spec[0])
                    p3flag = spec[1] if len(spec) > 1 else ""
                for ei in range(nexp):
                    routed = ei > 0
                    eix = ei - 1
                    wgh = wgt_h[eix] if routed else swgt_h
                    wuh = wut_h[eix] if routed else swut_h
                    wdh = wdt_h[eix] if routed else swdt_h
                    if routed and p3flag == "nocw":
                        routed = False
                    if routed:
                        p.emit("gpsimd", lambda e, eix=eix: e.dma_start(
                            cwrow[:], cwT_r[eix:eix + 1, :]),
                            waits=[("sV", max(cw_done, cwb_used))], inc="sX", incn=16)
                        cwrow_dma = p.cnt["sX"]
                        for tt in range(2):
                            p.emit("tensor", lambda e, tt=tt: e.matmul(
                                pb[5][:], ones_r[:], cwrow[:, tt * 512:(tt + 1) * 512],
                                start=True, stop=True),
                                waits=[("sX", cwrow_dma),
                                       ("sV", max(cwb_used, dband_free[3], p.cnt["sV"]))],
                                inc="sT")
                            p.emit("vector", lambda e, tt=tt: e.tensor_copy(
                                cw_b[:, tt * 512:(tt + 1) * 512], pb[5][:, 0:512]),
                                waits=[("sT", p.cnt["sT"])], inc="sV")
                        cwb_v = p.cnt["sV"]
                        dband_free[3] = max(dband_free[3], cwb_v)
                    else:
                        cwb_v = 0
                    for ic in range(8):
                        sg_w = 0 + (ic % 2)
                        su_w = 2 + (ic % 2)
                        for wi, whh, slot in ((0, wgh, sg_w), (1, wuh, su_w)):
                            wgt = wgb[slot]
                            for half in range(2):
                                wsb = ws[half]
                                p.emit("sync", lambda e, wsb=wsb, whh=whh, ic=ic, half=half: e.dma_start(
                                    wsb[:], whh[ic, :, half * 1024:(half + 1) * 1024]),
                                    waits=[("sV", ws_cast[half])], inc="sW", incn=16)
                                ldw_ = p.cnt["sW"]
                                p.emit("vector", lambda e, wgt=wgt, wsb=wsb, half=half: e.tensor_copy(
                                    wgt[:, half * 8:(half + 1) * 8, :].rearrange("p a b -> p (a b)"),
                                    wsb[:]),
                                    waits=[("sW", ldw_), ("sT", slot_mm_done[slot])], inc="sV")
                                ws_cast[half] = p.cnt["sV"]
                            slot_cast[slot] = p.cnt["sV"]
                        for kc in range(16):
                            for tt in range(2):
                                w_ = []
                                if kc == 0 and tt == 0:
                                    w_ = [("sV", slot_cast[sg_w]), ("sS", min(gpsum_free))]
                                p.emit("tensor", lambda e, tt=tt, kc=kc, sg_w=sg_w: e.matmul(
                                    pb[tt][:], wgb[sg_w][:, kc, :],
                                    xn2T[:, kc, tt * 512:(tt + 1) * 512],
                                    start=(kc == 0), stop=(kc == 15)),
                                    waits=w_, inc="sT" if (kc == 15 and tt == 1) else None)
                        gmm = p.cnt["sT"]
                        slot_mm_done[sg_w] = gmm
                        for kc in range(16):
                            for tt in range(2):
                                w_ = []
                                if kc == 0 and tt == 0:
                                    w_ = [("sV", slot_cast[su_w]),
                                          ("sV", max(min(upsum_free), dband_free[0], dband_free[1]))]
                                p.emit("tensor", lambda e, tt=tt, kc=kc, su_w=su_w: e.matmul(
                                    pb[2 + tt][:], wgb[su_w][:, kc, :],
                                    xn2T[:, kc, tt * 512:(tt + 1) * 512],
                                    start=(kc == 0), stop=(kc == 15)),
                                    waits=w_, inc="sT" if (kc == 15 and tt == 1) else None)
                        umm = p.cnt["sT"]
                        slot_mm_done[su_w] = umm
                        for tt in range(2):
                            p.emit("scalar", lambda e, tt=tt: e.activation(
                                sg[tt][:], pb[tt][:], AF.Silu),
                                waits=[("sT", gmm), ("sV", sg_free[tt])], inc="sS")
                            gpsum_free[tt] = p.cnt["sS"]
                            sgv = p.cnt["sS"]
                            hdst = hT[:, ic, tt * 512:(tt + 1) * 512]
                            if routed:
                                p.emit("vector", lambda e, tt=tt: e.tensor_tensor(
                                    t1b[:], pb[2 + tt][:], cw_b[:, tt * 512:(tt + 1) * 512],
                                    op=OP.mult),
                                    waits=[("sT", umm), ("sV", cwb_v), ("sT", hT_used)],
                                    inc="sV")
                                upsum_free[tt] = p.cnt["sV"]
                                p.emit("vector", lambda e, hdst=hdst, tt=tt: e.tensor_tensor(
                                    hdst, t1b[:], sg[tt][:], op=OP.mult),
                                    waits=[("sS", sgv)], inc="sV")
                            else:
                                p.emit("vector", lambda e, hdst=hdst, tt=tt: e.tensor_tensor(
                                    hdst, pb[2 + tt][:], sg[tt][:], op=OP.mult),
                                    waits=[("sT", umm), ("sS", sgv), ("sT", hT_used)],
                                    inc="sV")
                                upsum_free[tt] = p.cnt["sV"]
                            sg_free[tt] = p.cnt["sV"]
                            cwb_used = max(cwb_used, p.cnt["sV"])
                    h_done = p.cnt["sV"]
                    for ic in range(8):
                        for half in range(2):
                            wsb = ws[half]
                            p.emit("sync", lambda e, wsb=wsb, wdh=wdh, ic=ic, half=half: e.dma_start(
                                wsb[:], wdh[ic, :, half * 1024:(half + 1) * 1024]),
                                waits=[("sV", ws_cast[half])], inc="sW", incn=16)
                            ldd = p.cnt["sW"]
                            p.emit("vector", lambda e, ic=ic, half=half, wsb=wsb: e.tensor_copy(
                                wd3[:, ic, half * 1024:(half + 1) * 1024], wsb[:]),
                                waits=[("sW", ldd), ("sT", wd_used)], inc="sV")
                            ws_cast[half] = p.cnt["sV"]
                    wd_cast_done = p.cnt["sV"]
                    for c8 in range(8):
                        for ic in range(8):
                            for ho in range(4):
                                w_ = []
                                if ic == 0:
                                    w_ = [("sV", max(wd_cast_done, h_done, dband_free[ho]))]
                                p.emit("tensor", lambda e, ho=ho, ic=ic, c8=c8: e.matmul(
                                    pb[2 + ho][:], hT[:, ic, c8 * 128:(c8 + 1) * 128],
                                    wd3[:, ic, ho * 512:(ho + 1) * 512],
                                    start=(ic == 0), stop=(ic == 7)),
                                    waits=w_, inc="sT" if (ic == 7 and ho == 3) else None)
                        dmm = p.cnt["sT"]
                        for ho in range(4):
                            p.emit("vector", lambda e, ho=ho, c8=c8: e.tensor_tensor(
                                oa3[:, c8, ho * 512:(ho + 1) * 512],
                                oa3[:, c8, ho * 512:(ho + 1) * 512],
                                pb[2 + ho][:], op=OP.add),
                                waits=[("sT", dmm)], inc="sV")
                            dband_free[ho] = p.cnt["sV"]
                    hT_used = p.cnt["sT"]
                    wd_used = p.cnt["sT"]
                d_all = p.cnt["sV"]
                for c8 in range(8):
                    p.emit("gpsimd", lambda e, c8=c8, base=base: e.dma_start(
                        out_h[base + c8 * 128: base + (c8 + 1) * 128, :], oa3[:, c8, :]),
                        waits=[("sV", d_all)] if c8 == 0 else [], inc="sO", incn=16)
                out_dma_prev = p.cnt["sO"]
                if hf == 0:
                    p.barrier()
            p.emit("gpsimd", None, waits=[("sO", p.cnt["sO"])])
            p.flush()
    return nc



def _host_consts():
    d1 = np.zeros((128, 16, 512), np.float64)
    for di in range(16):
        delta = -1536 + di * 128
        pcol = np.arange(128)[:, None]
        ccol = np.arange(512)[None, :]
        d1[:, di, :] = -np.abs(delta + pcol - ccol) / 8.0
    d1t = d1.reshape(128, 16 * 512).astype(np.float16)

    inv_freq = (1.0 / 10000.0 ** (np.arange(0, DH, 2, dtype=np.float64) / DH))
    t = np.arange(S, dtype=np.float64)[:, None]
    ang = t * inv_freq[None, :]
    cosv = np.cos(ang)
    sinv = np.sin(ang)
    ropec = np.zeros((128, 4, 16, 32), np.float64)
    for tc in range(16):
        rows = slice(tc * 128, (tc + 1) * 128)
        ropec[:, 0, tc, :] = cosv[rows] / 8.0
        ropec[:, 1, tc, :] = sinv[rows] / 8.0
        ropec[:, 2, tc, :] = cosv[rows]
        ropec[:, 3, tc, :] = sinv[rows]
    ropec = ropec.reshape(128, 4 * 16 * 32).astype(np.float32)

    identb = np.eye(128, dtype=ml_dtypes.bfloat16)
    identf = np.eye(128, dtype=np.float32)
    r = np.arange(128)
    triu = (r[None, :] >= r[:, None]).astype(ml_dtypes.bfloat16)
    lowm = np.zeros((NE, NE), np.float32)
    for e in range(NE):
        lowm[e, :e] = 1.0
    lowm = np.broadcast_to(lowm.reshape(1, NE * NE), (128, NE * NE)).copy()
    return d1t, ropec, identb, identf, triu, lowm


def _img_lhsT(w):
    K, M = w.shape
    return np.ascontiguousarray(
        w.reshape(K // 128, 128, M // 128, 128).transpose(2, 1, 0, 3)
    ).reshape(M // 128, 128, K)


def kernel(L0_attn_hidden_states, L0_attn_Wq, L0_attn_Wk, L0_attn_Wv, L0_attn_Wo,
           L0_attn_bo, L0_ffn_gate_w, L0_ffn_expert_bias, L0_ffn_w_gate,
           L0_ffn_w_up, L0_ffn_w_down, L0_ffn_sw_gate, L0_ffn_sw_up,
           L0_ffn_sw_down):
    x = np.ascontiguousarray(np.asarray(L0_attn_hidden_states, dtype=np.float32))
    wq = np.ascontiguousarray(np.asarray(L0_attn_Wq, np.float32))
    wk = np.ascontiguousarray(np.asarray(L0_attn_Wk, np.float32))
    wv = np.ascontiguousarray(np.asarray(L0_attn_Wv, np.float32))
    wo = np.ascontiguousarray(np.asarray(L0_attn_Wo, np.float32))
    bo = np.asarray(L0_attn_bo, np.float32).reshape(1, D)
    gwt = np.ascontiguousarray(np.asarray(L0_ffn_gate_w, np.float32).T)
    eb = np.asarray(L0_ffn_expert_bias, np.float32).reshape(1, NE)
    wgT = np.stack([_img_lhsT(np.asarray(L0_ffn_w_gate[e], np.float32).T)
                    for e in range(NE)])
    wuT = np.stack([_img_lhsT(np.asarray(L0_ffn_w_up[e], np.float32).T)
                    for e in range(NE)])
    wdT = np.stack([np.ascontiguousarray(np.asarray(L0_ffn_w_down[e], np.float32).T)
                    .reshape(8, 128, D) for e in range(NE)])
    swgT = _img_lhsT(np.asarray(L0_ffn_sw_gate, np.float32).T)
    swuT = _img_lhsT(np.asarray(L0_ffn_sw_up, np.float32).T)
    swdT = np.ascontiguousarray(np.asarray(L0_ffn_sw_down, np.float32).T).reshape(8, 128, D)

    d1t, ropec, identb, identf, triu, lowm = _host_consts()

    nc = build()
    shared = {"wq": wq, "wk": wk, "wv": wv, "wo": wo, "bo": bo, "gwt": gwt,
              "ebias": eb, "wgT": wgT, "wuT": wuT, "wdT": wdT,
              "swgT": swgT, "swuT": swuT, "swdT": swdT,
              "d1t": d1t, "ropec": ropec, "identb": identb, "identf": identf,
              "triu": triu, "lowm": lowm}
    in_maps = [dict(shared, x=np.ascontiguousarray(x[b])) for b in range(8)]
    res = run_bass_kernel_spmd(nc, in_maps, core_ids=list(range(8)))
    out = np.stack([np.asarray(res.results[b]["out"], np.float32) for b in range(8)])
    return out
